# version 04
# baseline (speedup 1.0000x reference)
import math
import numpy as np
import ml_dtypes
from contextlib import ExitStack

import concourse.bass as bass
import concourse.mybir as mybir
from concourse.bass_utils import run_bass_kernel_spmd

F32 = mybir.dt.float32
BF16 = mybir.dt.bfloat16
AF = mybir.ActivationFunctionType
ALU = mybir.AluOpType
AX = mybir.AxisListType.X

D = 1024
DC = 8
HID = 4096
NHC = 32
NEXP = 8
EPS = 1e-5
HCB = 8
NHCB = NHC // HCB


class Sem:
    def __init__(self, ctx, nc, name):
        self.h = ctx.enter_context(nc.semaphore(name))
        self.n = 0


class Prog:

    def __init__(self):
        self.ops = {e: [] for e in ("sync", "gpsimd", "tensor", "vector", "scalar")}

    def em(self, eng, fn):
        self.ops[eng].append(fn)

    def wait(self, eng, sem, val):
        if val > 0:
            self.em(eng, lambda E, s=sem.h, v=val: E.wait_ge(s, v))

    def eop(self, eng, sem, build, serial=True):
        if serial and sem.n > 0:
            self.wait(eng, sem, sem.n)

        def f(E, build=build, s=sem.h):
            build(E).then_inc(s, 1)
        self.em(eng, f)
        sem.n += 1
        return sem.n

    def run_block(self, nc, name):
        with nc.Block(name) as block:
            for ename in self.ops:
                section = getattr(block, ename)
                opl = self.ops[ename]
                if not opl:
                    continue

                def body(E, opl=opl):
                    for f in opl:
                        f(E)
                section(body)
        self.ops = {e: [] for e in self.ops}


def build_nc(NT, seq_scale, debug=False, b3_serial=False):
    T = NT * 128
    NTS = T // 512

    nc = bass.Bass("TRN2", target_bir_lowering=False)

    x_d = nc.dram_tensor("x", [T, D], F32, kind="ExternalInput")
    qkvw_d = nc.dram_tensor("qkv_w", [3 * D, D], F32, kind="ExternalInput")
    qkvb_d = nc.dram_tensor("qkv_b", [3 * D], F32, kind="ExternalInput")
    nw_d = nc.dram_tensor("norm_w", [D], F32, kind="ExternalInput")
    nb_d = nc.dram_tensor("norm_b", [D], F32, kind="ExternalInput")
    pw_d = nc.dram_tensor("proj_w", [D, D], F32, kind="ExternalInput")
    pb_d = nc.dram_tensor("proj_b", [D], F32, kind="ExternalInput")
    gw_d = nc.dram_tensor("gate_w", [D, NEXP], F32, kind="ExternalInput")
    gb_d = nc.dram_tensor("gate_b", [NEXP], F32, kind="ExternalInput")
    w1_d = nc.dram_tensor("w1", [NEXP, D, HID], F32, kind="ExternalInput")
    b1_d = nc.dram_tensor("b1", [NEXP, HID], F32, kind="ExternalInput")
    w2_d = nc.dram_tensor("w2", [NEXP, HID, D], F32, kind="ExternalInput")
    b2_d = nc.dram_tensor("b2", [NEXP, D], F32, kind="ExternalInput")
    idb_d = nc.dram_tensor("ident_bf", [128, 128], BF16, kind="ExternalInput")
    idf_d = nc.dram_tensor("ident_f32", [128, 128], F32, kind="ExternalInput")
    y_d = nc.dram_tensor("y", [T, D], F32, kind="ExternalOutput")

    dbg = {}
    if debug:
        for nm, shp, dt in [("d_xnT", [128, DC * T], BF16), ("d_comb", [128, NT * NEXP], F32),
                            ("d_stat", [128, 2 * NT], F32), ("d_htT0", [128, DC * T], BF16),
                            ("d_htT", [128, DC * T], BF16), ("d_q", [128, NT * D], BF16),
                            ("d_k", [128, NT * D], BF16), ("d_v", [128, DC * T], BF16),
                            ("d_partl", [D, D], F32), ("d_redl", [D, D], F32),
                            ("d_accum", [128, NT * D], F32)]:
            dbg[nm] = nc.dram_tensor(nm, shp, dt, kind="ExternalOutput")
    partl_d = nc.dram_tensor("part_l", [D, D], F32)
    redl_d = nc.dram_tensor("red_l", [D, D], F32)
    vspill_d = nc.dram_tensor("v_spill", [128, DC * T], BF16)

    top = ExitStack()
    with top:
        s13 = ExitStack()
        xnT = s13.enter_context(nc.sbuf_tensor("xnT", [128, DC * T], BF16))
        comb = s13.enter_context(nc.sbuf_tensor("comb", [128, NT * NEXP], F32))
        combT = s13.enter_context(nc.sbuf_tensor("combT", [NEXP, T], BF16))
        qb_bc = s13.enter_context(nc.sbuf_tensor("qb_bc", [128, D], F32))
        kb_bc = s13.enter_context(nc.sbuf_tensor("kb_bc", [128, D], F32))
        rstd1 = top.enter_context(nc.sbuf_tensor("rstd1", [128, NT], F32, side="right"))
        nbias1 = top.enter_context(nc.sbuf_tensor("nbias1", [128, NT], F32, side="right"))
        projb_bc = top.enter_context(nc.sbuf_tensor("projb_bc", [128, D], F32, side="right"))
        b1c = top.enter_context(nc.sbuf_tensor("b1c", [128, NEXP * NHC], F32, side="right"))
        nw_c = top.enter_context(nc.sbuf_tensor("nw_c", [128, DC], F32, side="right"))
        nb_c = top.enter_context(nc.sbuf_tensor("nb_c", [128, DC], F32, side="right"))
        vb_c = top.enter_context(nc.sbuf_tensor("vb_c", [128, DC], F32, side="right"))
        gw_sb = top.enter_context(nc.sbuf_tensor("gw_sb", [128, DC * NEXP], BF16, side="right"))
        gb_bc = top.enter_context(nc.sbuf_tensor("gb_bc", [128, NEXP], F32, side="right"))
        b2_sb = s13.enter_context(nc.sbuf_tensor("b2_sb", [NEXP, D], BF16))
        idb = top.enter_context(nc.sbuf_tensor("idb", [128, 128], BF16, side="right"))
        idf = top.enter_context(nc.sbuf_tensor("idf", [128, 128], F32, side="right"))
        ones1 = top.enter_context(nc.sbuf_tensor("ones1", [1, 128], F32, side="right"))
        cst = top.enter_context(nc.sbuf_tensor("cst", [128, 4], F32, side="right"))
        nc.const_aps.register(F32, EPS, cst[:, 0:1])
        nc.const_aps.register(F32, -0.5, cst[:, 1:2])
        nc.const_aps.register(F32, seq_scale, cst[:, 2:3])

        sS = Sem(top, nc, "sS")
        sG = Sem(top, nc, "sG")
        sT = Sem(top, nc, "sT")
        sV = Sem(top, nc, "sV")
        sC = Sem(top, nc, "sC")
        sCC = Sem(top, nc, "sCC")
        sO = [Sem(top, nc, "sO0"), Sem(top, nc, "sO1")]
        sX = [Sem(top, nc, "sX0"), Sem(top, nc, "sX1")]
        sLB = [Sem(top, nc, "sLB0"), Sem(top, nc, "sLB1")]
        sSP = [Sem(top, nc, "sSP0"), Sem(top, nc, "sSP1")]
        sLC = [Sem(top, nc, "sLC0"), Sem(top, nc, "sLC1")]
        sX2 = [Sem(top, nc, "sX20"), Sem(top, nc, "sX21")]
        sVTL = Sem(top, nc, "sVTL")
        sW1 = [Sem(top, nc, "sW10"), Sem(top, nc, "sW11")]
        sW2 = [Sem(top, nc, "sW20"), Sem(top, nc, "sW21")]
        sWS = [Sem(top, nc, "sWS0"), Sem(top, nc, "sWS1")]
        sPST = Sem(top, nc, "sPST")

        def dump(P, name, srcap):
            if not debug:
                return
            P.wait("sync", sV, sV.n)
            P.wait("sync", sC, sC.n)
            P.wait("sync", sT, sT.n)
            def f(E, name=name, srcap=srcap):
                E.dma_start(out=dbg[name][:, :], in_=srcap).then_inc(sS.h, 16)
            P.em("sync", f)
            sS.n += 16
            P.wait("sync", sS, sS.n)

        with ExitStack() as b0:
            qrow = b0.enter_context(nc.sbuf_tensor("qrow", [1, D], F32))
            krow = b0.enter_context(nc.sbuf_tensor("krow", [1, D], F32))
            prow = b0.enter_context(nc.sbuf_tensor("prow", [1, D], F32))
            grow = b0.enter_context(nc.sbuf_tensor("grow", [1, NEXP], F32))
            nwst = b0.enter_context(nc.sbuf_tensor("nwst", [DC, 128], F32))
            nbst = b0.enter_context(nc.sbuf_tensor("nbst", [DC, 128], F32))
            vbst = b0.enter_context(nc.sbuf_tensor("vbst", [DC, 128], F32))
            b1st = b0.enter_context(nc.sbuf_tensor("b1st", [NHC, NEXP * 128], F32))
            psF = [b0.enter_context(nc.psum_tensor(f"psF{i}", [128, 512], F32)) for i in range(2)]
            psT1 = b0.enter_context(nc.psum_tensor("psT1", [128, NEXP * NHC], F32))

            P = Prog()
            loads = [
                (idb, idb_d[:, :]),
                (idf, idf_d[:, :]),
                (qrow, qkvb_d.ap()[0:D].rearrange("(a d) -> a d", a=1)),
                (krow, qkvb_d.ap()[D:2 * D].rearrange("(a d) -> a d", a=1)),
                (prow, pb_d.ap().rearrange("(a d) -> a d", a=1)),
                (grow, gb_d.ap().rearrange("(a d) -> a d", a=1)),
                (nwst, nw_d.ap().rearrange("(c p) -> c p", p=128)),
                (nbst, nb_d.ap().rearrange("(c p) -> c p", p=128)),
                (vbst, qkvb_d.ap()[2 * D:3 * D].rearrange("(c p) -> c p", p=128)),
            ] + [
                (b1st[:, bass.ts(e, 128)], b1_d.ap()[e].rearrange("(hc p) -> hc p", p=128))
                for e in range(NEXP)
            ]
            for dst, src in loads:
                def f(E, dst=dst, src=src):
                    E.dma_start(out=dst[:] if hasattr(dst, 'ap') else dst, in_=src).then_inc(sS.h, 16)
                P.em("sync", f)
                sS.n += 16
            lds_done = sS.n

            for dst, src in [(b2_sb, b2_d.ap()),
                             (gw_sb[:].rearrange("p (dc e) -> p dc e", dc=DC),
                              gw_d.ap().rearrange("(dc p) e -> p dc e", p=128))]:
                def f(E, dst=dst, src=src):
                    E.dma_start(out=dst[:] if hasattr(dst, 'ap') else dst, in_=src).then_inc(sG.h, 16)
                P.em("gpsimd", f)
                sG.n += 16

            def f(E):
                E.memset(ones1[:], 1.0)
                E.memset(cst[:, 0:1], EPS)
                E.memset(cst[:, 1:2], -0.5)
                E.memset(cst[:, 2:3], seq_scale).then_inc(sV.h, 1)
            P.em("vector", f)
            sV.n += 1

            P.wait("tensor", sS, lds_done)
            P.wait("tensor", sV, sV.n)

            psF_guard = [0, 0]
            for row_sb, dst, nhalf in [(qrow, qb_bc, 2), (krow, kb_bc, 2),
                                       (prow, projb_bc, 2), (grow, gb_bc, 0)]:
                if nhalf == 0:
                    P.wait("tensor", sV, psF_guard[0])

                    def fmm(E):
                        E.matmul(psF[0][:, 0:NEXP], ones1[0:1, :], grow[0:1, :],
                                 start=True, stop=True).then_inc(sT.h, 1)
                    P.em("tensor", fmm)
                    sT.n += 1
                    P.wait("vector", sT, sT.n)

                    def fcp(E):
                        E.tensor_copy(gb_bc[:, :], psF[0][:, 0:NEXP]).then_inc(sV.h, 1)
                    P.em("vector", fcp)
                    sV.n += 1
                    psF_guard[0] = sV.n
                    continue
                for half in range(2):
                    pi = half % 2
                    P.wait("tensor", sV, psF_guard[pi])

                    def fmm(E, pi=pi, row_sb=row_sb, half=half):
                        E.matmul(psF[pi][:, :], ones1[0:1, :], row_sb[0:1, bass.ts(half, 512)],
                                 start=True, stop=True).then_inc(sT.h, 1)
                    P.em("tensor", fmm)
                    sT.n += 1
                    P.wait("vector", sT, sT.n)

                    def fcp(E, pi=pi, dst=dst, half=half):
                        E.tensor_copy(dst[:, bass.ts(half, 512)], psF[pi][:]).then_inc(sV.h, 1)
                    P.em("vector", fcp)
                    sV.n += 1
                    psF_guard[pi] = sV.n

            psT1_guard = 0
            for src, dst, kk in [(nwst, nw_c, DC), (nbst, nb_c, DC), (vbst, vb_c, DC)]:
                P.wait("tensor", sV, psT1_guard)

                def fmm(E, src=src, kk=kk):
                    E.transpose(psT1[:, 0:kk], src[:, :], idf[0:kk, 0:kk]).then_inc(sT.h, 1)
                P.em("tensor", fmm)
                sT.n += 1
                P.wait("vector", sT, sT.n)

                def fcp(E, dst=dst, kk=kk):
                    E.tensor_copy(dst[:, :], psT1[:, 0:kk]).then_inc(sV.h, 1)
                P.em("vector", fcp)
                sV.n += 1
                psT1_guard = sV.n

            P.wait("tensor", sV, psT1_guard)
            for e in range(NEXP):
                def fmm(E, e=e):
                    E.transpose(psT1[:, bass.ts(e, NHC)], b1st[:, bass.ts(e, 128)],
                                idf[0:NHC, 0:NHC]).then_inc(sT.h, 1)
                P.em("tensor", fmm)
                sT.n += 1
            P.wait("vector", sT, sT.n)

            def fcp(E):
                E.tensor_copy(b1c[:, :], psT1[:, :]).then_inc(sV.h, 1)
            P.em("vector", fcp)
            sV.n += 1

            P.wait("sync", sS, sS.n)
            P.wait("gpsimd", sG, sG.n)
            P.run_block(nc, "b0")

        s12 = ExitStack()
        htT = s12.enter_context(nc.sbuf_tensor("htT", [128, DC * T], BF16))
        q_sb = s12.enter_context(nc.sbuf_tensor("q_sb", [128, NT * D], BF16))
        k_sb = s12.enter_context(nc.sbuf_tensor("k_sb", [128, NT * D], BF16))

        with ExitStack() as b1s:
            xbuf = [b1s.enter_context(nc.sbuf_tensor(f"xbuf{i}", [128, D], F32)) for i in range(2)]
            xnbuf = [b1s.enter_context(nc.sbuf_tensor(f"xnbuf{i}", [128, D], F32)) for i in range(2)]
            xsq = b1s.enter_context(nc.sbuf_tensor("xsq", [128, D], F32))
            xnbf = [b1s.enter_context(nc.sbuf_tensor(f"xnbf{i}", [128, D], BF16)) for i in range(2)]
            htbf = [b1s.enter_context(nc.sbuf_tensor(f"htbf{i}", [128, D], BF16)) for i in range(2)]
            stat = b1s.enter_context(nc.sbuf_tensor("stat", [128, 16], F32))
            gp = b1s.enter_context(nc.sbuf_tensor("gp", [128, NEXP], F32))
            gexp = b1s.enter_context(nc.sbuf_tensor("gexp", [128, NEXP], F32))
            gz = b1s.enter_context(nc.sbuf_tensor("gz", [128, NEXP], F32))
            gmask = b1s.enter_context(nc.sbuf_tensor("gmask", [128, NEXP], F32))
            combbf = b1s.enter_context(nc.sbuf_tensor("combbf", [128, NEXP], BF16))
            psA = [b1s.enter_context(nc.psum_tensor(f"psA{i}", [128, D], BF16)) for i in range(2)]
            psB = [b1s.enter_context(nc.psum_tensor(f"psB{i}", [128, D], BF16)) for i in range(2)]
            psGt = [b1s.enter_context(nc.psum_tensor(f"psGt{i}", [128, NEXP], F32)) for i in range(2)]
            psCt = [b1s.enter_context(nc.psum_tensor(f"psCt{i}", [NEXP, 128], BF16)) for i in range(2)]

            P = Prog()
            SUM, MU, EX2, VAR, LNV, RS2, NB2, TMP = range(8)

            psA_guard = [0, 0]
            psGt_guard = [0, 0]
            psCt_guard = [0, 0]
            xbuf_guard = [0, 0]
            xnbf_guard = [0, 0]
            V = lambda b: P.eop("vector", sV, b)
            S = lambda b: P.eop("scalar", sC, b)
            for i in range(NT):
                r = i % 2
                P.wait("sync", sC, xbuf_guard[r])

                def f(E, r=r, i=i):
                    E.dma_start(out=xbuf[r][:], in_=x_d[bass.ts(i, 128), :]).then_inc(sX[r].h, 16)
                P.em("sync", f)
                sX[r].n += 16

                P.wait("vector", sX[r], sX[r].n)
                V(lambda E, r=r: E.tensor_reduce(stat[:, SUM:SUM + 1], xbuf[r][:], AX, ALU.add))
                V(lambda E, r=r: E.tensor_tensor(xsq[:], xbuf[r][:], xbuf[r][:], ALU.mult))
                V(lambda E: E.tensor_reduce(stat[:, EX2:EX2 + 1], xsq[:], AX, ALU.add))
                V(lambda E: E.tensor_scalar(stat[:, MU:MU + 1], stat[:, SUM:SUM + 1], 1.0 / D, None, ALU.mult))
                V(lambda E: E.tensor_scalar(stat[:, EX2:EX2 + 1], stat[:, EX2:EX2 + 1], 1.0 / D, None, ALU.mult))
                V(lambda E: E.tensor_tensor(stat[:, TMP:TMP + 1], stat[:, MU:MU + 1], stat[:, MU:MU + 1], ALU.mult))
                V(lambda E: E.tensor_tensor(stat[:, VAR:VAR + 1], stat[:, EX2:EX2 + 1], stat[:, TMP:TMP + 1], ALU.subtract))

                P.wait("scalar", sV, sV.n)
                S(lambda E: E.activation(stat[:, LNV:LNV + 1], stat[:, VAR:VAR + 1], AF.Ln, bias=EPS, scale=1.0))
                S(lambda E, i=i: E.activation(rstd1[:, i:i + 1], stat[:, LNV:LNV + 1], AF.Exp, bias=0.0, scale=-0.5))

                P.wait("vector", sC, sC.n)
                V(lambda E, i=i: E.tensor_scalar(nbias1[:, i:i + 1], stat[:, MU:MU + 1], rstd1[:, i:i + 1], -1.0, ALU.mult, ALU.mult))

                P.wait("scalar", sV, sV.n)
                S(lambda E, r=r, i=i: E.activation(xnbuf[r][:], xbuf[r][:], AF.Identity, bias=nbias1[:, i:i + 1], scale=rstd1[:, i:i + 1]))
                xbuf_guard[r] = sC.n

                P.wait("vector", sC, sC.n)
                V(lambda E, r=r: E.tensor_reduce(stat[:, SUM:SUM + 1], xnbuf[r][:], AX, ALU.add))
                V(lambda E, r=r: E.tensor_tensor(xsq[:], xnbuf[r][:], xnbuf[r][:], ALU.mult))
                V(lambda E: E.tensor_reduce(stat[:, EX2:EX2 + 1], xsq[:], AX, ALU.add))
                V(lambda E: E.tensor_scalar(stat[:, MU:MU + 1], stat[:, SUM:SUM + 1], 1.0 / D, None, ALU.mult))
                V(lambda E: E.tensor_scalar(stat[:, EX2:EX2 + 1], stat[:, EX2:EX2 + 1], 1.0 / D, None, ALU.mult))
                V(lambda E: E.tensor_tensor(stat[:, TMP:TMP + 1], stat[:, MU:MU + 1], stat[:, MU:MU + 1], ALU.mult))
                V(lambda E: E.tensor_tensor(stat[:, VAR:VAR + 1], stat[:, EX2:EX2 + 1], stat[:, TMP:TMP + 1], ALU.subtract))

                P.wait("scalar", sV, sV.n)
                S(lambda E: E.activation(stat[:, LNV:LNV + 1], stat[:, VAR:VAR + 1], AF.Ln, bias=EPS, scale=1.0))
                S(lambda E: E.activation(stat[:, RS2:RS2 + 1], stat[:, LNV:LNV + 1], AF.Exp, bias=0.0, scale=-0.5))

                P.wait("vector", sC, sC.n)
                V(lambda E: E.tensor_scalar(stat[:, NB2:NB2 + 1], stat[:, MU:MU + 1], stat[:, RS2:RS2 + 1], -1.0, ALU.mult, ALU.mult))

                P.wait("scalar", sT, xnbf_guard[r])
                P.wait("scalar", sV, sV.n)
                S(lambda E, r=r: E.copy(xnbf[r][:], xnbuf[r][:]))
                S(lambda E, r=r: E.activation(htbf[r][:], xnbuf[r][:], AF.Identity, bias=stat[:, NB2:NB2 + 1], scale=stat[:, RS2:RS2 + 1]))

                P.wait("tensor", sC, sC.n)
                P.wait("tensor", sV, psA_guard[r])

                def f(E, r=r):
                    for j in range(DC):
                        E.transpose(psA[r][:, bass.ts(j, 128)], xnbf[r][:, bass.ts(j, 128)], idb[:])
                    mm = None
                    for j in range(DC):
                        mm = E.transpose(psB[r][:, bass.ts(j, 128)], htbf[r][:, bass.ts(j, 128)], idb[:])
                    mm.then_inc(sT.h, 1)
                P.em("tensor", f)
                sT.n += 1
                xnbf_guard[r] = sT.n

                P.wait("vector", sT, sT.n)
                V(lambda E, r=r, i=i: E.tensor_copy(
                    xnT[:].rearrange("p (j t) -> p j t", j=DC)[:, :, bass.ts(i, 128)],
                    psA[r][:].rearrange("p (j c) -> p j c", j=DC)))
                V(lambda E, r=r, i=i: E.tensor_copy(
                    htT[:].rearrange("p (j t) -> p j t", j=DC)[:, :, bass.ts(i, 128)],
                    psB[r][:].rearrange("p (j c) -> p j c", j=DC)))
                cp_done = sV.n
                psA_guard[r] = cp_done

                P.wait("tensor", sV, cp_done)
                P.wait("tensor", sV, psGt_guard[r])

                def f(E, r=r, i=i):
                    mm = None
                    for dc in range(DC):
                        mm = E.matmul(psGt[r][:, :], xnT[:, dc * T + i * 128:dc * T + (i + 1) * 128],
                                      gw_sb[:, bass.ts(dc, NEXP)], start=(dc == 0), stop=(dc == DC - 1))
                    mm.then_inc(sT.h, 1)
                P.em("tensor", f)
                sT.n += 1

                P.wait("vector", sT, sT.n)
                V(lambda E, r=r: E.tensor_tensor(gp[:], psGt[r][:], gb_bc[:], ALU.add))
                psGt_guard[r] = sV.n
                V(lambda E: E.tensor_reduce(stat[:, TMP:TMP + 1], gp[:], AX, ALU.max))
                V(lambda E: E.tensor_scalar(stat[:, TMP:TMP + 1], stat[:, TMP:TMP + 1], -1.0, None, ALU.mult))

                P.wait("scalar", sV, sV.n)
                S(lambda E: E.activation(gexp[:], gp[:], AF.Exp, bias=stat[:, TMP:TMP + 1], scale=1.0))

                P.wait("vector", sC, sC.n)
                V(lambda E: E.tensor_reduce(stat[:, SUM:SUM + 1], gexp[:], AX, ALU.add))
                V(lambda E: E.reciprocal(stat[:, SUM:SUM + 1], stat[:, SUM:SUM + 1]))
                V(lambda E: E.tensor_scalar(gp[:], gexp[:], stat[:, SUM:SUM + 1], None, ALU.mult))
                V(lambda E: E.tensor_reduce(stat[:, MU:MU + 1], gp[:], AX, ALU.max))
                V(lambda E: E.tensor_scalar(gmask[:], gp[:], stat[:, MU:MU + 1], None, ALU.is_lt))
                V(lambda E: E.tensor_tensor(gz[:], gp[:], gmask[:], ALU.mult))
                V(lambda E: E.tensor_reduce(stat[:, EX2:EX2 + 1], gz[:], AX, ALU.max))
                V(lambda E: E.tensor_tensor(stat[:, VAR:VAR + 1], stat[:, MU:MU + 1], stat[:, EX2:EX2 + 1], ALU.add))
                V(lambda E: E.reciprocal(stat[:, VAR:VAR + 1], stat[:, VAR:VAR + 1]))
                V(lambda E: E.tensor_scalar(gmask[:], gp[:], stat[:, EX2:EX2 + 1], None, ALU.is_ge))
                V(lambda E, i=i: E.scalar_tensor_tensor(comb[:, bass.ts(i, NEXP)], gp[:], stat[:, VAR:VAR + 1], gmask[:], ALU.mult, ALU.mult))
                V(lambda E, i=i: E.tensor_copy(combbf[:], comb[:, bass.ts(i, NEXP)]))

                P.wait("tensor", sV, sV.n)
                P.wait("tensor", sV, psCt_guard[r])

                def f(E, r=r):
                    E.transpose(psCt[r][:, :], combbf[:], idb[:]).then_inc(sT.h, 1)
                P.em("tensor", f)
                sT.n += 1

                P.wait("vector", sT, sT.n)
                V(lambda E, r=r, i=i: E.tensor_copy(combT[:, bass.ts(i, 128)], psCt[r][:, :]))
                psCt_guard[r] = sV.n

            dump(P, "d_xnT", xnT[:])
            dump(P, "d_htT0", htT[:])
            dump(P, "d_comb", comb[:])
            def fstat(E):
                E.dma_start(out=dbg["d_stat"][:, 0:NT], in_=rstd1[:]).then_inc(sS.h, 16)
                E.dma_start(out=dbg["d_stat"][:, NT:2 * NT], in_=nbias1[:]).then_inc(sS.h, 16)
            if debug:
                P.wait("sync", sC, sC.n)
                P.em("sync", fstat)
                sS.n += 32
                P.wait("sync", sS, sS.n)
            P.run_block(nc, "b1")

        with ExitStack() as b2a:
            wst = [b2a.enter_context(nc.sbuf_tensor(f"wst{i}", [128, 4 * D], BF16)) for i in range(2)]
            wT_sec = b2a.enter_context(nc.sbuf_tensor("wT_sec", [128, DC * D], BF16))
            vtc = [b2a.enter_context(nc.sbuf_tensor(f"vtc{i}", [128, T], BF16)) for i in range(2)]
            psW = [b2a.enter_context(nc.psum_tensor(f"psW{i}", [128, D], BF16)) for i in range(2)]
            psQ = [b2a.enter_context(nc.psum_tensor(f"psQ{i}", [128, 512], F32)) for i in range(4)]

            P = Prog()

            def f(E):
                mm = None
                for dc in range(DC):
                    mm = E.activation(htT[:, bass.ts(dc, T)], htT[:, bass.ts(dc, T)], AF.Identity,
                                      bias=nb_c[:, dc:dc + 1], scale=nw_c[:, dc:dc + 1])
                mm.then_inc(sC.h, 1)
            P.em("scalar", f)
            sC.n += 1
            hT_ready = sC.n

            wst_guard = [0, 0]
            psW_guard = [0, 0]
            psQ_guard_v = [0, 0, 0, 0]
            psQ_guard_c = [0, 0, 0, 0]
            spill_guard = [0, 0]
            for sec in range(3):
                for hf in range(2):
                    b = hf % 2
                    P.wait("gpsimd", sT, wst_guard[b])

                    def f(E, sec=sec, hf=hf, b=b):
                        rows0 = sec * D + hf * 512
                        src = qkvw_d.ap()[rows0:rows0 + 512, :].rearrange(
                            "(oc p) d -> p oc d", p=128)
                        dst = wst[b][:].rearrange("p (oc d) -> p oc d", oc=4)
                        E.dma_start(out=dst, in_=src).then_inc(sWS[b].h, 16)
                    P.em("gpsimd", f)
                    sWS[b].n += 16

                    P.wait("tensor", sWS[b], sWS[b].n)
                    for oc in range(4):
                        pr = oc % 2
                        P.wait("tensor", sV, psW_guard[pr])

                        def f(E, b=b, oc=oc, pr=pr):
                            mm = None
                            for dc in range(DC):
                                mm = E.transpose(psW[pr][:, bass.ts(dc, 128)],
                                                 wst[b][:, oc * D + dc * 128:oc * D + (dc + 1) * 128],
                                                 idb[:])
                            mm.then_inc(sT.h, 1)
                        P.em("tensor", f)
                        sT.n += 1
                        if oc == 3:
                            wst_guard[b] = sT.n

                        P.wait("vector", sT, sT.n)

                        def f(E, pr=pr, hf=hf, oc=oc):
                            oabs = hf * 4 + oc
                            E.tensor_copy(
                                wT_sec[:].rearrange("p (j o) -> p j o", j=DC)[:, :, bass.ts(oabs, 128)],
                                psW[pr][:].rearrange("p (j c) -> p j c", j=DC)).then_inc(sV.h, 1)
                        P.em("vector", f)
                        sV.n += 1
                        psW_guard[pr] = sV.n
                wT_ready_v = sV.n

                P.wait("tensor", sV, wT_ready_v)
                P.wait("tensor", sC, hT_ready)
                if sec < 2:
                    dst_sb = q_sb if sec == 0 else k_sb
                    bias_bc = qb_bc if sec == 0 else kb_bc
                    for i in range(NT):
                        for oh in range(2):
                            pq = (i * 2 + oh) % 4
                            P.wait("tensor", sV, psQ_guard_v[pq])

                            def f(E, i=i, oh=oh, pq=pq):
                                mm = None
                                for dc in range(DC):
                                    mm = E.matmul(psQ[pq][:, :],
                                                  htT[:, dc * T + i * 128:dc * T + (i + 1) * 128],
                                                  wT_sec[:, dc * D + oh * 512:dc * D + (oh + 1) * 512],
                                                  start=(dc == 0), stop=(dc == DC - 1))
                                mm.then_inc(sT.h, 1)
                            P.em("tensor", f)
                            sT.n += 1

                            P.wait("vector", sT, sT.n)

                            def f(E, i=i, oh=oh, pq=pq, dst_sb=dst_sb, bias_bc=bias_bc):
                                E.scalar_tensor_tensor(
                                    dst_sb[:, i * D + oh * 512:i * D + (oh + 1) * 512],
                                    psQ[pq][:], 1.0, bias_bc[:, bass.ts(oh, 512)],
                                    ALU.mult, ALU.add).then_inc(sV.h, 1)
                            P.em("vector", f)
                            sV.n += 1
                            psQ_guard_v[pq] = sV.n
                else:
                    P.wait("tensor", sV, max(psQ_guard_v))
                    for kkc in range(DC):
                        vb = kkc % 2
                        for ts in range(NTS):
                            pq = ts % 4
                            P.wait("tensor", sC, psQ_guard_c[pq])

                            def f(E, kkc=kkc, ts=ts, pq=pq):
                                mm = None
                                for dc in range(DC):
                                    mm = E.matmul(psQ[pq][:, :],
                                                  wT_sec[:, dc * D + kkc * 128:dc * D + (kkc + 1) * 128],
                                                  htT[:, dc * T + ts * 512:dc * T + (ts + 1) * 512],
                                                  start=(dc == 0), stop=(dc == DC - 1))
                                mm.then_inc(sT.h, 1)
                            P.em("tensor", f)
                            sT.n += 1

                            P.wait("scalar", sT, sT.n)
                            P.wait("scalar", sSP[vb], spill_guard[vb])

                            def f(E, kkc=kkc, ts=ts, pq=pq, vb=vb):
                                E.activation(vtc[vb][:, bass.ts(ts, 512)], psQ[pq][:], AF.Identity,
                                             bias=vb_c[:, kkc:kkc + 1], scale=1.0).then_inc(sC.h, 1)
                            P.em("scalar", f)
                            sC.n += 1
                            psQ_guard_c[pq] = sC.n

                        P.wait("sync", sC, sC.n)

                        def f(E, kkc=kkc, vb=vb):
                            E.dma_start(out=vspill_d.ap()[:, bass.ts(kkc, T)],
                                        in_=vtc[vb][:]).then_inc(sSP[vb].h, 16)
                        P.em("sync", f)
                        sSP[vb].n += 16
                        spill_guard[vb] = sSP[vb].n

            dump(P, "d_htT", htT[:])
            dump(P, "d_q", q_sb[:])
            dump(P, "d_k", k_sb[:])
            if debug:
                P.wait("sync", sSP[0], sSP[0].n)
                P.wait("sync", sSP[1], sSP[1].n)
                dump(P, "d_v", vspill_d.ap()[:, :])
            P.run_block(nc, "b2a")

        with ExitStack() as b2b:
            lb = [b2b.enter_context(nc.sbuf_tensor(f"lb{i}", [128, D], F32)) for i in range(2)]
            psL = [b2b.enter_context(nc.psum_tensor(f"psL{i}", [128, D], F32)) for i in range(4)]

            P = Prog()
            lb_guard = [0, 0]
            for p in range(2):
                for qc in range(4):
                    def f(E, p=p, qc=qc):
                        mm = None
                        for i in range(NT):
                            for kh in range(2):
                                mm = E.matmul(psL[qc][:, bass.ts(kh, 512)],
                                              q_sb[:, i * D + p * 512 + qc * 128:
                                                   i * D + p * 512 + (qc + 1) * 128],
                                              k_sb[:, i * D + kh * 512:i * D + (kh + 1) * 512],
                                              start=(i == 0), stop=(i == NT - 1))
                        mm.then_inc(sT.h, 1)
                    P.em("tensor", f)
                    sT.n += 1

                    b = qc % 2
                    P.wait("vector", sT, sT.n)
                    P.wait("vector", sLB[b], lb_guard[b])

                    def f(E, qc=qc, b=b):
                        E.tensor_copy(lb[b][:], psL[qc][:]).then_inc(sV.h, 1)
                    P.em("vector", f)
                    sV.n += 1

                    P.wait("sync", sV, sV.n)

                    def f(E, p=p, qc=qc, b=b):
                        r0 = p * 512 + qc * 128
                        E.dma_start(out=partl_d.ap()[r0:r0 + 128, :], in_=lb[b][:]).then_inc(sLB[b].h, 16)
                    P.em("sync", f)
                    sLB[b].n += 16
                    lb_guard[b] = sLB[b].n
                if p == 0:
                    P.wait("tensor", sV, sV.n)

            P.wait("gpsimd", sLB[0], sLB[0].n)
            P.wait("gpsimd", sLB[1], sLB[1].n)

            def f(E):
                E.collective_compute(
                    "AllReduce", ALU.add,
                    replica_groups=[[0, 1, 2, 3], [4, 5, 6, 7]],
                    ins=[partl_d.ap().opt()],
                    outs=[redl_d.ap().opt()],
                ).then_inc(sCC.h, 1)
            P.em("gpsimd", f)
            sCC.n += 1

            dump(P, "d_partl", partl_d.ap()[:, :])
            P.run_block(nc, "b2b")

        s12.close()

        accum = top.enter_context(nc.sbuf_tensor("accum", [128, NT * D], F32, side="right"))
        with ExitStack() as b3:
            hT = b3.enter_context(nc.sbuf_tensor("hT", [128, HCB * T], BF16))
            w2b = [b3.enter_context(nc.sbuf_tensor(f"w2b{i}", [128, HCB * D], BF16)) for i in range(2)]
            w1c = [b3.enter_context(nc.sbuf_tensor(f"w1c{i}", [128, DC * 128], BF16)) for i in range(2)]
            psH = [b3.enter_context(nc.psum_tensor(f"psH{i}", [128, 512], F32)) for i in range(4)]
            psY = [b3.enter_context(nc.psum_tensor(f"psY{i}", [128, D], F32)) for i in range(2)]

            P = Prog()
            NSUB = T // 512
            psY_guard = [0, 0]
            psH_guard = [0, 0, 0, 0]
            w1_guard = [0, 0]
            w2_guard = [0, 0]
            accum_guard = [0] * NT

            for tt in range(NT):
                py = tt % 2
                P.wait("tensor", sV, psY_guard[py])

                def f(E, tt=tt, py=py):
                    mm = None
                    for dh in range(2):
                        mm = E.matmul(psY[py][:, bass.ts(dh, 512)], combT[:, bass.ts(tt, 128)],
                                      b2_sb[:, bass.ts(dh, 512)], start=True, stop=True)
                    mm.then_inc(sT.h, 1)
                P.em("tensor", f)
                sT.n += 1

                P.wait("vector", sT, sT.n)

                def f(E, tt=tt, py=py):
                    E.tensor_copy(accum[:, bass.ts(tt, D)], psY[py][:]).then_inc(sV.h, 1)
                P.em("vector", f)
                sV.n += 1
                psY_guard[py] = sV.n
                accum_guard[tt] = sV.n

            for e in range(NEXP):
                for hb in range(NHCB):
                    blk = e * NHCB + hb
                    wb2 = blk % 2
                    P.wait("gpsimd", sT, w2_guard[wb2])

                    def f(E, e=e, hb=hb, wb2=wb2):
                        r0 = hb * HCB * 128
                        src = w2_d.ap()[e][r0:r0 + HCB * 128, :].rearrange(
                            "(hc p) d -> p hc d", p=128)
                        dst = w2b[wb2][:].rearrange("p (hc d) -> p hc d", hc=HCB)
                        E.dma_start(out=dst, in_=src).then_inc(sW2[wb2].h, 16)
                    P.em("gpsimd", f)
                    sW2[wb2].n += 16
                    w2_ld = sW2[wb2].n

                    for hc in range(HCB):
                        wb = hc % 2
                        P.wait("gpsimd", sT, w1_guard[wb])

                        def f(E, e=e, hb=hb, hc=hc, wb=wb):
                            c0 = hb * HCB * 128 + hc * 128
                            src = w1_d.ap()[e][:, c0:c0 + 128].rearrange(
                                "(dc p) h -> p dc h", p=128)
                            dst = w1c[wb][:].rearrange("p (dc h) -> p dc h", dc=DC)
                            E.dma_start(out=dst, in_=src).then_inc(sW1[wb].h, 16)
                        P.em("gpsimd", f)
                        sW1[wb].n += 16

                        P.wait("tensor", sW1[wb], sW1[wb].n)
                        for sub in range(NSUB):
                            ph = (hc * NSUB + sub) % 4
                            P.wait("tensor", sC, psH_guard[ph])
                            if b3_serial:
                                P.wait("tensor", sC, sC.n)
                                P.wait("tensor", sV, sV.n)

                            def f(E, hc=hc, sub=sub, ph=ph, wb=wb):
                                mm = None
                                for dc in range(DC):
                                    mm = E.matmul(psH[ph][:, :], w1c[wb][:, bass.ts(dc, 128)],
                                                  xnT[:, dc * T + sub * 512:dc * T + (sub + 1) * 512],
                                                  start=(dc == 0), stop=(dc == DC - 1))
                                mm.then_inc(sT.h, 1)
                            P.em("tensor", f)
                            sT.n += 1
                            if sub == NSUB - 1:
                                w1_guard[wb] = sT.n

                            P.wait("scalar", sT, sT.n)
                            if b3_serial:
                                P.wait("scalar", sC, sC.n)

                            def f(E, e=e, hb=hb, hc=hc, sub=sub, ph=ph):
                                bcol = e * NHC + hb * HCB + hc
                                E.activation(hT[:, hc * T + sub * 512:hc * T + (sub + 1) * 512],
                                             psH[ph][:], AF.Silu, bias=b1c[:, bcol:bcol + 1],
                                             scale=1.0).then_inc(sC.h, 1)
                            P.em("scalar", f)
                            sC.n += 1
                            psH_guard[ph] = sC.n

                    P.wait("tensor", sC, sC.n)
                    P.wait("tensor", sW2[wb2], w2_ld)
                    for tt in range(NT):
                        py = tt % 2
                        P.wait("tensor", sV, psY_guard[py])
                        if b3_serial:
                            P.wait("tensor", sV, sV.n)
                            P.wait("tensor", sC, sC.n)

                        def f(E, tt=tt, py=py, wb2=wb2):
                            mm = None
                            for hc in range(HCB):
                                for dh in range(2):
                                    mm = E.matmul(psY[py][:, bass.ts(dh, 512)],
                                                  hT[:, hc * T + tt * 128:hc * T + (tt + 1) * 128],
                                                  w2b[wb2][:, hc * D + dh * 512:hc * D + (dh + 1) * 512],
                                                  start=(hc == 0), stop=(hc == HCB - 1))
                            mm.then_inc(sT.h, 1)
                        P.em("tensor", f)
                        sT.n += 1

                        P.wait("vector", sT, sT.n)
                        P.wait("vector", sV, accum_guard[tt])
                        if b3_serial:
                            P.wait("vector", sV, sV.n)

                        def f(E, e=e, tt=tt, py=py):
                            E.scalar_tensor_tensor(accum[:, bass.ts(tt, D)], psY[py][:],
                                                   comb[:, tt * NEXP + e:tt * NEXP + e + 1],
                                                   accum[:, bass.ts(tt, D)],
                                                   ALU.mult, ALU.add).then_inc(sV.h, 1)
                        P.em("vector", f)
                        sV.n += 1
                        psY_guard[py] = sV.n
                        accum_guard[tt] = sV.n
                    w2_guard[wb2] = sT.n

            dump(P, "d_accum", accum[:])
            if debug:
                P.wait("sync", sCC, 1)
                dump(P, "d_redl", redl_d.ap()[:, :])
            P.run_block(nc, "b3")

        s13.close()

        with ExitStack() as b4:
            vT = b4.enter_context(nc.sbuf_tensor("vT", [128, DC * T], BF16))
            wT = b4.enter_context(nc.sbuf_tensor("wT", [128, DC * D], BF16))
            aT = b4.enter_context(nc.sbuf_tensor("aT", [128, DC * T], BF16))
            pwT = b4.enter_context(nc.sbuf_tensor("pwT", [128, DC * D], BF16))
            pst = b4.enter_context(nc.sbuf_tensor("pst", [128, 2 * D], BF16))
            lc = [b4.enter_context(nc.sbuf_tensor(f"lc{i}", [128, D], F32)) for i in range(2)]
            wbf = b4.enter_context(nc.sbuf_tensor("wbf", [128, D], BF16))
            x2 = [b4.enter_context(nc.sbuf_tensor(f"x2{i}", [128, D], F32)) for i in range(2)]
            xn2 = b4.enter_context(nc.sbuf_tensor("xn2", [128, D], F32))
            gbuf = b4.enter_context(nc.sbuf_tensor("gbuf", [128, D], F32))
            obuf = [b4.enter_context(nc.sbuf_tensor(f"obuf{i}", [128, D], F32)) for i in range(2)]
            sstat = b4.enter_context(nc.sbuf_tensor("sstat", [128, 4], F32))
            psW2 = [b4.enter_context(nc.psum_tensor(f"psW2{i}", [128, D], BF16)) for i in range(2)]
            psA2 = [b4.enter_context(nc.psum_tensor(f"psA2{i}", [128, 512], F32)) for i in range(2)]
            psP = [b4.enter_context(nc.psum_tensor(f"psP{i}", [128, D], F32)) for i in range(2)]

            P = Prog()

            P.wait("sync", sSP[0], sSP[0].n)
            P.wait("sync", sSP[1], sSP[1].n)

            def f(E):
                E.dma_start(out=vT[:], in_=vspill_d.ap()[:, :]).then_inc(sVTL.h, 16)
            P.em("sync", f)
            sVTL.n += 16

            P.wait("sync", sCC, 1)
            V = lambda b: P.eop("vector", sV, b)
            S = lambda b: P.eop("scalar", sC, b)
            lc_guard = [0, 0]
            psW2_guard = [0, 0]
            wbf_guard = [0]
            for qc in range(DC):
                b = qc % 2
                P.wait("sync", sV, lc_guard[b])

                def f(E, qc=qc, b=b):
                    E.dma_start(out=lc[b][:], in_=redl_d.ap()[bass.ts(qc, 128), :]).then_inc(sLC[b].h, 16)
                P.em("sync", f)
                sLC[b].n += 16

                P.wait("vector", sLC[b], sLC[b].n)
                P.wait("vector", sC, sC.n)
                V(lambda E, b=b: E.tensor_reduce(sstat[:, 0:1], lc[b][:], AX, ALU.max))
                V(lambda E: E.tensor_scalar(sstat[:, 0:1], sstat[:, 0:1], -seq_scale, None, ALU.mult))

                P.wait("scalar", sV, sV.n)
                S(lambda E, b=b: E.activation(lc[b][:], lc[b][:], AF.Exp, bias=sstat[:, 0:1], scale=seq_scale))

                P.wait("vector", sC, sC.n)
                V(lambda E, b=b: E.tensor_reduce(sstat[:, 1:2], lc[b][:], AX, ALU.add))
                V(lambda E: E.reciprocal(sstat[:, 1:2], sstat[:, 1:2]))
                P.wait("vector", sT, wbf_guard[0])
                V(lambda E, b=b: E.tensor_scalar(wbf[:], lc[b][:], sstat[:, 1:2], None, ALU.mult))
                lc_guard[b] = sV.n

                pb = qc % 2
                P.wait("tensor", sV, sV.n)
                P.wait("tensor", sV, psW2_guard[pb])

                def f(E, pb=pb):
                    mm = None
                    for kkc in range(DC):
                        mm = E.transpose(psW2[pb][:, bass.ts(kkc, 128)],
                                         wbf[:, bass.ts(kkc, 128)], idb[:])
                    mm.then_inc(sT.h, 1)
                P.em("tensor", f)
                sT.n += 1
                wtr = sT.n
                wbf_guard[0] = wtr

                P.wait("vector", sT, wtr)
                V(lambda E, pb=pb, qc=qc: E.tensor_copy(
                    wT[:].rearrange("p (j q) -> p j q", j=DC)[:, :, bass.ts(qc, 128)],
                    psW2[pb][:].rearrange("p (j c) -> p j c", j=DC)))
                psW2_guard[pb] = sV.n
            wT_done_v = sV.n

            pst_guard = 0
            for hf in range(4):
                P.wait("gpsimd", sT, pst_guard)

                def f(E, hf=hf):
                    src = pw_d.ap()[hf * 256:(hf + 1) * 256, :].rearrange(
                        "(oc p) q -> p oc q", p=128)
                    dst = pst[:].rearrange("p (oc q) -> p oc q", oc=2)
                    E.dma_start(out=dst, in_=src).then_inc(sPST.h, 16)
                P.em("gpsimd", f)
                sPST.n += 16

                P.wait("tensor", sPST, sPST.n)
                for oc in range(2):
                    pb = oc % 2
                    P.wait("tensor", sV, psW2_guard[pb])

                    def f(E, oc=oc, pb=pb):
                        mm = None
                        for qc in range(DC):
                            mm = E.transpose(psW2[pb][:, bass.ts(qc, 128)],
                                             pst[:, oc * D + qc * 128:oc * D + (qc + 1) * 128],
                                             idb[:])
                        mm.then_inc(sT.h, 1)
                    P.em("tensor", f)
                    sT.n += 1
                    if oc == 1:
                        pst_guard = sT.n

                    P.wait("vector", sT, sT.n)
                    P.eop("vector", sV, lambda E, hf=hf, oc=oc, pb=pb: E.tensor_copy(
                        pwT[:].rearrange("p (j o) -> p j o", j=DC)[:, :, bass.ts(hf * 2 + oc, 128)],
                        psW2[pb][:].rearrange("p (j c) -> p j c", j=DC)))
                    psW2_guard[pb] = sV.n
            pwT_done_v = sV.n

            P.wait("tensor", sVTL, sVTL.n)
            P.wait("tensor", sV, wT_done_v)
            psA2_guard = [0, 0]
            for qc in range(DC):
                for ts in range(NTS):
                    pa = (qc * NTS + ts) % 2
                    P.wait("tensor", sC, psA2_guard[pa])

                    def f(E, qc=qc, ts=ts, pa=pa):
                        mm = None
                        for kkc in range(DC):
                            mm = E.matmul(psA2[pa][:, :],
                                          wT[:, kkc * D + qc * 128:kkc * D + (qc + 1) * 128],
                                          vT[:, kkc * T + ts * 512:kkc * T + (ts + 1) * 512],
                                          start=(kkc == 0), stop=(kkc == DC - 1))
                        mm.then_inc(sT.h, 1)
                    P.em("tensor", f)
                    sT.n += 1

                    P.wait("scalar", sT, sT.n)

                    def f(E, qc=qc, ts=ts, pa=pa):
                        E.copy(aT[:, qc * T + ts * 512:qc * T + (ts + 1) * 512],
                               psA2[pa][:]).then_inc(sC.h, 1)
                    P.em("scalar", f)
                    sC.n += 1
                    psA2_guard[pa] = sC.n
            aT_done_c = sC.n

            P.wait("tensor", sC, aT_done_c)
            P.wait("tensor", sV, pwT_done_v)
            x2_guard = [0, 0]
            psP_guard = [0, 0]
            obuf_guard = [0, 0]
            prev_o = 0
            for i in range(NT):
                r = i % 2
                P.wait("sync", sV, x2_guard[r])

                def f(E, i=i, r=r):
                    E.dma_start(out=x2[r][:], in_=x_d[bass.ts(i, 128), :]).then_inc(sX2[r].h, 16)
                P.em("sync", f)
                sX2[r].n += 16
                x_ld = sX2[r].n

                P.wait("tensor", sV, psP_guard[r])

                def f(E, i=i, r=r):
                    mm = None
                    for qc in range(DC):
                        for dh in range(2):
                            mm = E.matmul(psP[r][:, bass.ts(dh, 512)],
                                          aT[:, qc * T + i * 128:qc * T + (i + 1) * 128],
                                          pwT[:, qc * D + dh * 512:qc * D + (dh + 1) * 512],
                                          start=(qc == 0), stop=(qc == DC - 1))
                    mm.then_inc(sT.h, 1)
                P.em("tensor", f)
                sT.n += 1
                pmm = sT.n

                P.wait("scalar", sX2[r], x_ld)
                P.wait("scalar", sV, prev_o)

                S(lambda E, i=i, r=r: E.activation(xn2[:], x2[r][:], AF.Identity,
                                                   bias=nbias1[:, i:i + 1], scale=rstd1[:, i:i + 1]))
                S(lambda E, i=i: E.activation(gbuf[:], accum[:, bass.ts(i, D)], AF.Silu,
                                              bias=0.0, scale=1.0))

                P.wait("vector", sT, pmm)
                P.wait("vector", sC, sC.n)
                P.wait("vector", sO[r], obuf_guard[r])
                V(lambda E, r=r: E.scalar_tensor_tensor(xn2[:], psP[r][:], 1.0, xn2[:], ALU.mult, ALU.add))
                V(lambda E: E.tensor_tensor(xn2[:], xn2[:], projb_bc[:], ALU.add))
                V(lambda E: E.tensor_tensor(xn2[:], xn2[:], gbuf[:], ALU.mult))
                V(lambda E, r=r: E.tensor_tensor(obuf[r][:], xn2[:], x2[r][:], ALU.add))
                o_done = sV.n
                psP_guard[r] = o_done
                x2_guard[r] = o_done
                prev_o = o_done

                P.wait("sync", sV, o_done)

                def f(E, i=i, r=r):
                    E.dma_start(out=y_d[bass.ts(i, 128), :], in_=obuf[r][:]).then_inc(sO[r].h, 16)
                P.em("sync", f)
                sO[r].n += 16
                obuf_guard[r] = sO[r].n

            P.wait("sync", sO[0], sO[0].n)
            P.wait("sync", sO[1], sO[1].n)

            P.run_block(nc, "b4")

    return nc


def kernel(**inputs):
    x = np.ascontiguousarray(inputs["L0_attn_x"])
    B, L, _ = x.shape
    n_cores = 8
    per_batch = n_cores // B
    Tc = L // per_batch

    weights = {
        "qkv_w": np.ascontiguousarray(inputs["L0_attn_qkv_w"], dtype=np.float32),
        "qkv_b": np.ascontiguousarray(inputs["L0_attn_qkv_b"], dtype=np.float32),
        "norm_w": np.ascontiguousarray(inputs["L0_attn_norm_w"], dtype=np.float32),
        "norm_b": np.ascontiguousarray(inputs["L0_attn_norm_b"], dtype=np.float32),
        "proj_w": np.ascontiguousarray(inputs["L0_attn_proj_w"], dtype=np.float32),
        "proj_b": np.ascontiguousarray(inputs["L0_attn_proj_b"], dtype=np.float32),
        "gate_w": np.ascontiguousarray(inputs["L0_ffn_gate_w"], dtype=np.float32),
        "gate_b": np.ascontiguousarray(inputs["L0_ffn_gate_b"], dtype=np.float32),
        "w1": np.ascontiguousarray(inputs["L0_ffn_w1"], dtype=np.float32),
        "b1": np.ascontiguousarray(inputs["L0_ffn_b1"], dtype=np.float32),
        "w2": np.ascontiguousarray(inputs["L0_ffn_w2"], dtype=np.float32),
        "b2": np.ascontiguousarray(inputs["L0_ffn_b2"], dtype=np.float32),
        "ident_bf": np.eye(128, dtype=ml_dtypes.bfloat16),
        "ident_f32": np.eye(128, dtype=np.float32),
    }

    nc = build_nc(Tc // 128, 1.0 / math.sqrt(L))

    in_maps = []
    for c in range(n_cores):
        b = c // per_batch
        s0 = (c % per_batch) * Tc
        m = dict(weights)
        m["x"] = np.ascontiguousarray(x[b, s0:s0 + Tc, :], dtype=np.float32)
        in_maps.append(m)

    res = run_bass_kernel_spmd(nc, in_maps, core_ids=list(range(n_cores)))
    out = np.empty((B, L, D), dtype=np.float32)
    for c in range(n_cores):
        b = c // per_batch
        s0 = (c % per_batch) * Tc
        out[b, s0:s0 + Tc, :] = res.results[c]["y"]
    return out


# version 06
# speedup vs baseline: 1.0360x; 1.0360x over previous
import math
import numpy as np
import ml_dtypes
from contextlib import ExitStack

import concourse.bass as bass
import concourse.mybir as mybir
from concourse.bass_utils import run_bass_kernel_spmd

F32 = mybir.dt.float32
BF16 = mybir.dt.bfloat16
AF = mybir.ActivationFunctionType
ALU = mybir.AluOpType
AX = mybir.AxisListType.X

D = 1024
DC = 8
HID = 4096
NHC = 32
NEXP = 8
EPS = 1e-5
HCB = 8
NHCB = NHC // HCB


class Sem:
    def __init__(self, ctx, nc, name):
        self.h = ctx.enter_context(nc.semaphore(name))
        self.n = 0


class Prog:

    def __init__(self):
        self.ops = {e: [] for e in ("sync", "gpsimd", "tensor", "vector", "scalar")}

    def em(self, eng, fn):
        self.ops[eng].append(fn)

    def wait(self, eng, sem, val):
        if val > 0:
            self.em(eng, lambda E, s=sem.h, v=val: E.wait_ge(s, v))

    def eop(self, eng, sem, build, serial=True):
        if serial and sem.n > 0:
            self.wait(eng, sem, sem.n)

        def f(E, build=build, s=sem.h):
            build(E).then_inc(s, 1)
        self.em(eng, f)
        sem.n += 1
        return sem.n

    def run_block(self, nc, name):
        with nc.Block(name) as block:
            for ename in self.ops:
                section = getattr(block, ename)
                opl = self.ops[ename]
                if not opl:
                    continue

                def body(E, opl=opl):
                    for f in opl:
                        f(E)
                section(body)
        self.ops = {e: [] for e in self.ops}


def build_nc(NT, seq_scale, debug=False, b3_serial=False):
    T = NT * 128
    NTS = T // 512

    nc = bass.Bass("TRN2", target_bir_lowering=False)

    x_d = nc.dram_tensor("x", [T, D], F32, kind="ExternalInput")
    qkvw_d = nc.dram_tensor("qkv_w", [3 * D, D], F32, kind="ExternalInput")
    qkvb_d = nc.dram_tensor("qkv_b", [3 * D], F32, kind="ExternalInput")
    nw_d = nc.dram_tensor("norm_w", [D], F32, kind="ExternalInput")
    nb_d = nc.dram_tensor("norm_b", [D], F32, kind="ExternalInput")
    pw_d = nc.dram_tensor("proj_w", [D, D], F32, kind="ExternalInput")
    pb_d = nc.dram_tensor("proj_b", [D], F32, kind="ExternalInput")
    gw_d = nc.dram_tensor("gate_w", [D, NEXP], F32, kind="ExternalInput")
    gb_d = nc.dram_tensor("gate_b", [NEXP], F32, kind="ExternalInput")
    w1_d = nc.dram_tensor("w1", [NEXP, D, HID], F32, kind="ExternalInput")
    b1_d = nc.dram_tensor("b1", [NEXP, HID], F32, kind="ExternalInput")
    w2_d = nc.dram_tensor("w2", [NEXP, HID, D], F32, kind="ExternalInput")
    b2_d = nc.dram_tensor("b2", [NEXP, D], F32, kind="ExternalInput")
    idb_d = nc.dram_tensor("ident_bf", [128, 128], BF16, kind="ExternalInput")
    idf_d = nc.dram_tensor("ident_f32", [128, 128], F32, kind="ExternalInput")
    y_d = nc.dram_tensor("y", [T, D], F32, kind="ExternalOutput")

    dbg = {}
    if debug:
        for nm, shp, dt in [("d_xnT", [128, DC * T], BF16), ("d_comb", [128, NT * NEXP], F32),
                            ("d_stat", [128, 2 * NT], F32), ("d_htT0", [128, DC * T], BF16),
                            ("d_htT", [128, DC * T], BF16), ("d_q", [128, NT * D], BF16),
                            ("d_k", [128, NT * D], BF16), ("d_v", [128, DC * T], BF16),
                            ("d_partl", [D, D], F32), ("d_redl", [D, D], F32),
                            ("d_accum", [128, NT * D], F32)]:
            dbg[nm] = nc.dram_tensor(nm, shp, dt, kind="ExternalOutput")
    partl_d = nc.dram_tensor("part_l", [D, D], F32)
    redl_d = nc.dram_tensor("red_l", [D, D], F32)
    vspill_d = nc.dram_tensor("v_spill", [128, DC * T], BF16)

    top = ExitStack()
    with top:
        s13 = ExitStack()
        xnT = s13.enter_context(nc.sbuf_tensor("xnT", [128, DC * T], BF16))
        comb = s13.enter_context(nc.sbuf_tensor("comb", [128, NT * NEXP], F32))
        combT = s13.enter_context(nc.sbuf_tensor("combT", [NEXP, T], BF16))
        qb_bc = s13.enter_context(nc.sbuf_tensor("qb_bc", [128, D], F32))
        kb_bc = s13.enter_context(nc.sbuf_tensor("kb_bc", [128, D], F32))
        rstd1 = top.enter_context(nc.sbuf_tensor("rstd1", [128, NT], F32, side="right"))
        nbias1 = top.enter_context(nc.sbuf_tensor("nbias1", [128, NT], F32, side="right"))
        projb_bc = top.enter_context(nc.sbuf_tensor("projb_bc", [128, D], F32, side="right"))
        b1c = top.enter_context(nc.sbuf_tensor("b1c", [128, NEXP * NHC], F32, side="right"))
        nw_c = top.enter_context(nc.sbuf_tensor("nw_c", [128, DC], F32, side="right"))
        nb_c = top.enter_context(nc.sbuf_tensor("nb_c", [128, DC], F32, side="right"))
        vb_c = top.enter_context(nc.sbuf_tensor("vb_c", [128, DC], F32, side="right"))
        gw_sb = top.enter_context(nc.sbuf_tensor("gw_sb", [128, DC * NEXP], BF16, side="right"))
        gb_bc = top.enter_context(nc.sbuf_tensor("gb_bc", [128, NEXP], F32, side="right"))
        b2_sb = s13.enter_context(nc.sbuf_tensor("b2_sb", [NEXP, D], BF16))
        idb = top.enter_context(nc.sbuf_tensor("idb", [128, 128], BF16, side="right"))
        idf = top.enter_context(nc.sbuf_tensor("idf", [128, 128], F32, side="right"))
        ones1 = top.enter_context(nc.sbuf_tensor("ones1", [1, 128], F32, side="right"))
        cst = top.enter_context(nc.sbuf_tensor("cst", [128, 4], F32, side="right"))
        nc.const_aps.register(F32, EPS, cst[:, 0:1])
        nc.const_aps.register(F32, -0.5, cst[:, 1:2])
        nc.const_aps.register(F32, seq_scale, cst[:, 2:3])
        nc.const_aps.register(F32, 1.0 / D, cst[:, 3:4])

        sS = Sem(top, nc, "sS")
        sG = Sem(top, nc, "sG")
        sT = Sem(top, nc, "sT")
        sV = Sem(top, nc, "sV")
        sC = Sem(top, nc, "sC")
        sCC = Sem(top, nc, "sCC")
        sO = [Sem(top, nc, "sO0"), Sem(top, nc, "sO1")]
        sX = [Sem(top, nc, "sX0"), Sem(top, nc, "sX1")]
        sLB = [Sem(top, nc, "sLB0"), Sem(top, nc, "sLB1")]
        sSP = [Sem(top, nc, "sSP0"), Sem(top, nc, "sSP1")]
        sLC = [Sem(top, nc, "sLC0"), Sem(top, nc, "sLC1")]
        sX2 = [Sem(top, nc, "sX20"), Sem(top, nc, "sX21")]
        sVTL = Sem(top, nc, "sVTL")
        sW1 = [Sem(top, nc, "sW10"), Sem(top, nc, "sW11")]
        sW2 = [Sem(top, nc, "sW20"), Sem(top, nc, "sW21")]
        sWS = [Sem(top, nc, "sWS0"), Sem(top, nc, "sWS1")]
        sPST = Sem(top, nc, "sPST")

        def dump(P, name, srcap):
            if not debug:
                return
            P.wait("sync", sV, sV.n)
            P.wait("sync", sC, sC.n)
            P.wait("sync", sT, sT.n)
            def f(E, name=name, srcap=srcap):
                E.dma_start(out=dbg[name][:, :], in_=srcap).then_inc(sS.h, 16)
            P.em("sync", f)
            sS.n += 16
            P.wait("sync", sS, sS.n)

        with ExitStack() as b0:
            qrow = b0.enter_context(nc.sbuf_tensor("qrow", [1, D], F32))
            krow = b0.enter_context(nc.sbuf_tensor("krow", [1, D], F32))
            prow = b0.enter_context(nc.sbuf_tensor("prow", [1, D], F32))
            grow = b0.enter_context(nc.sbuf_tensor("grow", [1, NEXP], F32))
            nwst = b0.enter_context(nc.sbuf_tensor("nwst", [DC, 128], F32))
            nbst = b0.enter_context(nc.sbuf_tensor("nbst", [DC, 128], F32))
            vbst = b0.enter_context(nc.sbuf_tensor("vbst", [DC, 128], F32))
            b1st = b0.enter_context(nc.sbuf_tensor("b1st", [NHC, NEXP * 128], F32))
            psF = [b0.enter_context(nc.psum_tensor(f"psF{i}", [128, 512], F32)) for i in range(2)]
            psT1 = b0.enter_context(nc.psum_tensor("psT1", [128, NEXP * NHC], F32))

            P = Prog()
            loads = [
                (idb, idb_d[:, :]),
                (idf, idf_d[:, :]),
                (qrow, qkvb_d.ap()[0:D].rearrange("(a d) -> a d", a=1)),
                (krow, qkvb_d.ap()[D:2 * D].rearrange("(a d) -> a d", a=1)),
                (prow, pb_d.ap().rearrange("(a d) -> a d", a=1)),
                (grow, gb_d.ap().rearrange("(a d) -> a d", a=1)),
                (nwst, nw_d.ap().rearrange("(c p) -> c p", p=128)),
                (nbst, nb_d.ap().rearrange("(c p) -> c p", p=128)),
                (vbst, qkvb_d.ap()[2 * D:3 * D].rearrange("(c p) -> c p", p=128)),
            ] + [
                (b1st[:, bass.ts(e, 128)], b1_d.ap()[e].rearrange("(hc p) -> hc p", p=128))
                for e in range(NEXP)
            ]
            for dst, src in loads:
                def f(E, dst=dst, src=src):
                    E.dma_start(out=dst[:] if hasattr(dst, 'ap') else dst, in_=src).then_inc(sS.h, 16)
                P.em("sync", f)
                sS.n += 16
            lds_done = sS.n

            for dst, src in [(b2_sb, b2_d.ap()),
                             (gw_sb[:].rearrange("p (dc e) -> p dc e", dc=DC),
                              gw_d.ap().rearrange("(dc p) e -> p dc e", p=128))]:
                def f(E, dst=dst, src=src):
                    E.dma_start(out=dst[:] if hasattr(dst, 'ap') else dst, in_=src).then_inc(sG.h, 16)
                P.em("gpsimd", f)
                sG.n += 16

            def f(E):
                E.memset(ones1[:], 1.0)
                E.memset(cst[:, 0:1], EPS)
                E.memset(cst[:, 1:2], -0.5)
                E.memset(cst[:, 2:3], seq_scale)
                E.memset(cst[:, 3:4], 1.0 / D).then_inc(sV.h, 1)
            P.em("vector", f)
            sV.n += 1

            P.wait("tensor", sS, lds_done)
            P.wait("tensor", sV, sV.n)

            psF_guard = [0, 0]
            for row_sb, dst, nhalf in [(qrow, qb_bc, 2), (krow, kb_bc, 2),
                                       (prow, projb_bc, 2), (grow, gb_bc, 0)]:
                if nhalf == 0:
                    P.wait("tensor", sV, psF_guard[0])

                    def fmm(E):
                        E.matmul(psF[0][:, 0:NEXP], ones1[0:1, :], grow[0:1, :],
                                 start=True, stop=True).then_inc(sT.h, 1)
                    P.em("tensor", fmm)
                    sT.n += 1
                    P.wait("vector", sT, sT.n)

                    def fcp(E):
                        E.tensor_copy(gb_bc[:, :], psF[0][:, 0:NEXP]).then_inc(sV.h, 1)
                    P.em("vector", fcp)
                    sV.n += 1
                    psF_guard[0] = sV.n
                    continue
                for half in range(2):
                    pi = half % 2
                    P.wait("tensor", sV, psF_guard[pi])

                    def fmm(E, pi=pi, row_sb=row_sb, half=half):
                        E.matmul(psF[pi][:, :], ones1[0:1, :], row_sb[0:1, bass.ts(half, 512)],
                                 start=True, stop=True).then_inc(sT.h, 1)
                    P.em("tensor", fmm)
                    sT.n += 1
                    P.wait("vector", sT, sT.n)

                    def fcp(E, pi=pi, dst=dst, half=half):
                        E.tensor_copy(dst[:, bass.ts(half, 512)], psF[pi][:]).then_inc(sV.h, 1)
                    P.em("vector", fcp)
                    sV.n += 1
                    psF_guard[pi] = sV.n

            psT1_guard = 0
            for src, dst, kk in [(nwst, nw_c, DC), (nbst, nb_c, DC), (vbst, vb_c, DC)]:
                P.wait("tensor", sV, psT1_guard)

                def fmm(E, src=src, kk=kk):
                    E.transpose(psT1[:, 0:kk], src[:, :], idf[0:kk, 0:kk]).then_inc(sT.h, 1)
                P.em("tensor", fmm)
                sT.n += 1
                P.wait("vector", sT, sT.n)

                def fcp(E, dst=dst, kk=kk):
                    E.tensor_copy(dst[:, :], psT1[:, 0:kk]).then_inc(sV.h, 1)
                P.em("vector", fcp)
                sV.n += 1
                psT1_guard = sV.n

            P.wait("tensor", sV, psT1_guard)
            for e in range(NEXP):
                def fmm(E, e=e):
                    E.transpose(psT1[:, bass.ts(e, NHC)], b1st[:, bass.ts(e, 128)],
                                idf[0:NHC, 0:NHC]).then_inc(sT.h, 1)
                P.em("tensor", fmm)
                sT.n += 1
            P.wait("vector", sT, sT.n)

            def fcp(E):
                E.tensor_copy(b1c[:, :], psT1[:, :]).then_inc(sV.h, 1)
            P.em("vector", fcp)
            sV.n += 1

            P.wait("sync", sS, sS.n)
            P.wait("gpsimd", sG, sG.n)
            P.run_block(nc, "b0")

        s12 = ExitStack()
        htT = s12.enter_context(nc.sbuf_tensor("htT", [128, DC * T], BF16))
        q_sb = s12.enter_context(nc.sbuf_tensor("q_sb", [128, NT * D], BF16))
        k_sb = s12.enter_context(nc.sbuf_tensor("k_sb", [128, NT * D], BF16))

        NQ = NT // 4
        with ExitStack() as b1s:
            xquad = [b1s.enter_context(nc.sbuf_tensor("xquad0", [128, 4 * D], F32))]
            xnquad = b1s.enter_context(nc.sbuf_tensor("xnquad", [128, 4 * D], F32))
            sdump = b1s.enter_context(nc.sbuf_tensor("sdump", [128, D], F32))
            xnbf = [b1s.enter_context(nc.sbuf_tensor(f"xnbf{i}", [128, D], BF16)) for i in range(2)]
            htbf = [b1s.enter_context(nc.sbuf_tensor(f"htbf{i}", [128, D], BF16)) for i in range(2)]
            ssum = b1s.enter_context(nc.sbuf_tensor("ssum", [128, NT], F32))
            ssq = b1s.enter_context(nc.sbuf_tensor("ssq", [128, NT], F32))
            ssum2 = b1s.enter_context(nc.sbuf_tensor("ssum2", [128, NT], F32))
            ssq2 = b1s.enter_context(nc.sbuf_tensor("ssq2", [128, NT], F32))
            rs2q = b1s.enter_context(nc.sbuf_tensor("rs2q", [128, NT], F32))
            nb2q = b1s.enter_context(nc.sbuf_tensor("nb2q", [128, NT], F32))
            scr1 = b1s.enter_context(nc.sbuf_tensor("scr1", [128, NT], F32))
            scr2 = b1s.enter_context(nc.sbuf_tensor("scr2", [128, NT], F32))
            gall = b1s.enter_context(nc.sbuf_tensor("gall", [128, NT * NEXP], F32))
            gm1 = b1s.enter_context(nc.sbuf_tensor("gm1", [128, NT], F32))
            gm2 = b1s.enter_context(nc.sbuf_tensor("gm2", [128, NT], F32))
            gmask8 = b1s.enter_context(nc.sbuf_tensor("gmask8", [128, NEXP], F32))
            gz8 = b1s.enter_context(nc.sbuf_tensor("gz8", [128, NEXP], F32))
            combbf = b1s.enter_context(nc.sbuf_tensor("combbf", [128, NEXP], BF16))
            psA = [b1s.enter_context(nc.psum_tensor(f"psA{i}", [128, D], BF16)) for i in range(2)]
            psB = [b1s.enter_context(nc.psum_tensor(f"psB{i}", [128, D], BF16)) for i in range(2)]
            psGt = [b1s.enter_context(nc.psum_tensor(f"psGt{i}", [128, NEXP], F32)) for i in range(2)]
            psCt = [b1s.enter_context(nc.psum_tensor(f"psCt{i}", [NEXP, 128], BF16)) for i in range(2)]

            P = Prog()
            V = lambda b: P.eop("vector", sV, b)
            S = lambda b: P.eop("scalar", sC, b)

            psA_guard = [0, 0]
            psGt_guard = [0, 0]
            psCt_guard = [0, 0]
            xnbf_guard = [0, 0]
            xquad_guard = [0, 0]
            for q in range(NQ):
                r = 0
                qs = slice(q * 4, q * 4 + 4)
                P.wait("sync", sC, xquad_guard[r])

                def f(E, q=q, r=r):
                    srcap = x_d.ap()[q * 512:(q + 1) * 512, :].rearrange("(tt p) d -> p tt d", p=128)
                    dst = xquad[r][:].rearrange("p (tt d) -> p tt d", tt=4)
                    E.dma_start(out=dst, in_=srcap).then_inc(sX[r].h, 16)
                P.em("sync", f)
                sX[r].n += 16

                P.wait("scalar", sX[r], sX[r].n)
                for j in range(4):
                    i = q * 4 + j
                    S(lambda E, r=r, j=j, i=i: E.activation(sdump[:], xquad[r][:, bass.ts(j, D)],
                                                            AF.Identity, bias=0.0, scale=1.0,
                                                            accum_out=ssum[:, i:i + 1]))
                    S(lambda E, r=r, j=j, i=i: E.activation(sdump[:], xquad[r][:, bass.ts(j, D)],
                                                            AF.Square, bias=0.0, scale=1.0,
                                                            accum_out=ssq[:, i:i + 1]))

                P.wait("vector", sC, sC.n)
                V(lambda E, qs=qs: E.tensor_tensor(scr1[:, qs], ssum[:, qs], ssum[:, qs], ALU.mult))
                V(lambda E, qs=qs: E.scalar_tensor_tensor(scr2[:, qs], scr1[:, qs], -1.0 / D,
                                                          ssq[:, qs], ALU.mult, ALU.add))
                P.wait("scalar", sV, sV.n)
                S(lambda E, qs=qs: E.activation(scr1[:, qs], scr2[:, qs], AF.Ln, bias=EPS, scale=1.0 / D))
                S(lambda E, qs=qs: E.activation(rstd1[:, qs], scr1[:, qs], AF.Exp, bias=0.0, scale=-0.5))
                P.wait("vector", sC, sC.n)
                V(lambda E, qs=qs: E.scalar_tensor_tensor(nbias1[:, qs], ssum[:, qs], -1.0 / D,
                                                          rstd1[:, qs], ALU.mult, ALU.mult))

                P.wait("scalar", sV, sV.n)
                for j in range(4):
                    i = q * 4 + j
                    S(lambda E, r=r, j=j, i=i: E.activation(xnquad[:, bass.ts(j, D)],
                                                            xquad[r][:, bass.ts(j, D)], AF.Identity,
                                                            bias=nbias1[:, i:i + 1], scale=rstd1[:, i:i + 1],
                                                            accum_out=ssum2[:, i:i + 1]))
                    S(lambda E, j=j, i=i: E.activation(sdump[:], xnquad[:, bass.ts(j, D)],
                                                       AF.Square, bias=0.0, scale=1.0,
                                                       accum_out=ssq2[:, i:i + 1]))
                xquad_guard[r] = sC.n

                P.wait("vector", sC, sC.n)
                V(lambda E, qs=qs: E.tensor_tensor(scr1[:, qs], ssum2[:, qs], ssum2[:, qs], ALU.mult))
                V(lambda E, qs=qs: E.scalar_tensor_tensor(scr2[:, qs], scr1[:, qs], -1.0 / D,
                                                          ssq2[:, qs], ALU.mult, ALU.add))
                P.wait("scalar", sV, sV.n)
                S(lambda E, qs=qs: E.activation(scr1[:, qs], scr2[:, qs], AF.Ln, bias=EPS, scale=1.0 / D))
                S(lambda E, qs=qs: E.activation(rs2q[:, qs], scr1[:, qs], AF.Exp, bias=0.0, scale=-0.5))
                P.wait("vector", sC, sC.n)
                V(lambda E, qs=qs: E.scalar_tensor_tensor(nb2q[:, qs], ssum2[:, qs], -1.0 / D,
                                                          rs2q[:, qs], ALU.mult, ALU.mult))

                P.wait("scalar", sV, sV.n)
                for j in range(4):
                    i = q * 4 + j
                    rj = i % 2
                    P.wait("scalar", sT, xnbf_guard[rj])
                    S(lambda E, rj=rj, j=j: E.copy(xnbf[rj][:], xnquad[:, bass.ts(j, D)]))
                    S(lambda E, rj=rj, j=j, i=i: E.activation(htbf[rj][:], xnquad[:, bass.ts(j, D)],
                                                              AF.Identity, bias=nb2q[:, i:i + 1],
                                                              scale=rs2q[:, i:i + 1]))

                    P.wait("tensor", sC, sC.n)
                    P.wait("tensor", sV, psA_guard[rj])

                    def f(E, rj=rj):
                        for jj in range(DC):
                            E.transpose(psA[rj][:, bass.ts(jj, 128)], xnbf[rj][:, bass.ts(jj, 128)], idb[:])
                        mm = None
                        for jj in range(DC):
                            mm = E.transpose(psB[rj][:, bass.ts(jj, 128)], htbf[rj][:, bass.ts(jj, 128)], idb[:])
                        mm.then_inc(sT.h, 1)
                    P.em("tensor", f)
                    sT.n += 1
                    xnbf_guard[rj] = sT.n

                    P.wait("vector", sT, sT.n)
                    V(lambda E, rj=rj, i=i: E.tensor_copy(
                        xnT[:].rearrange("p (jj t) -> p jj t", jj=DC)[:, :, bass.ts(i, 128)],
                        psA[rj][:].rearrange("p (jj c) -> p jj c", jj=DC)))
                    V(lambda E, rj=rj, i=i: E.tensor_copy(
                        htT[:].rearrange("p (jj t) -> p jj t", jj=DC)[:, :, bass.ts(i, 128)],
                        psB[rj][:].rearrange("p (jj c) -> p jj c", jj=DC)))
                    cp_done = sV.n
                    psA_guard[rj] = cp_done

                    P.wait("tensor", sV, cp_done)
                    P.wait("tensor", sV, psGt_guard[rj])

                    def f(E, rj=rj, i=i):
                        mm = None
                        for dc in range(DC):
                            mm = E.matmul(psGt[rj][:, :], xnT[:, dc * T + i * 128:dc * T + (i + 1) * 128],
                                          gw_sb[:, bass.ts(dc, NEXP)], start=(dc == 0), stop=(dc == DC - 1))
                        mm.then_inc(sT.h, 1)
                    P.em("tensor", f)
                    sT.n += 1

                    P.wait("vector", sT, sT.n)
                    V(lambda E, rj=rj, i=i: E.scalar_tensor_tensor(gall[:, bass.ts(i, NEXP)], psGt[rj][:],
                                                                   1.0, gb_bc[:], ALU.mult, ALU.add))
                    psGt_guard[rj] = sV.n

                    P.wait("scalar", sV, sV.n)
                    S(lambda E, i=i: E.activation(gall[:, bass.ts(i, NEXP)], gall[:, bass.ts(i, NEXP)],
                                                  AF.Exp, bias=0.0, scale=1.0))

                    P.wait("vector", sC, sC.n)
                    V(lambda E, i=i: E.tensor_reduce(gm1[:, i:i + 1], gall[:, bass.ts(i, NEXP)], AX, ALU.max))
                    V(lambda E, i=i: E.tensor_scalar(gmask8[:], gall[:, bass.ts(i, NEXP)],
                                                     gm1[:, i:i + 1], None, ALU.is_lt))
                    V(lambda E, i=i: E.tensor_tensor(gz8[:], gall[:, bass.ts(i, NEXP)], gmask8[:], ALU.mult))
                    V(lambda E, i=i: E.tensor_reduce(gm2[:, i:i + 1], gz8[:], AX, ALU.max))
                    V(lambda E, i=i: E.tensor_tensor(scr1[:, i:i + 1], gm1[:, i:i + 1], gm2[:, i:i + 1], ALU.add))
                    V(lambda E, i=i: E.reciprocal(scr1[:, i:i + 1], scr1[:, i:i + 1]))
                    V(lambda E, i=i: E.tensor_scalar(gmask8[:], gall[:, bass.ts(i, NEXP)],
                                                     gm2[:, i:i + 1], None, ALU.is_ge))
                    V(lambda E, i=i: E.scalar_tensor_tensor(comb[:, bass.ts(i, NEXP)], gall[:, bass.ts(i, NEXP)],
                                                            scr1[:, i:i + 1], gmask8[:], ALU.mult, ALU.mult))
                    V(lambda E, i=i: E.tensor_copy(combbf[:], comb[:, bass.ts(i, NEXP)]))

                    P.wait("tensor", sV, sV.n)
                    P.wait("tensor", sV, psCt_guard[rj])

                    def f(E, rj=rj):
                        E.transpose(psCt[rj][:, :], combbf[:], idb[:]).then_inc(sT.h, 1)
                    P.em("tensor", f)
                    sT.n += 1

                    P.wait("vector", sT, sT.n)
                    V(lambda E, rj=rj, i=i: E.tensor_copy(combT[:, bass.ts(i, 128)], psCt[rj][:, :]))
                    psCt_guard[rj] = sV.n

            dump(P, "d_xnT", xnT[:])
            dump(P, "d_htT0", htT[:])
            dump(P, "d_comb", comb[:])
            def fstat(E):
                E.dma_start(out=dbg["d_stat"][:, 0:NT], in_=rstd1[:]).then_inc(sS.h, 16)
                E.dma_start(out=dbg["d_stat"][:, NT:2 * NT], in_=nbias1[:]).then_inc(sS.h, 16)
            if debug:
                P.wait("sync", sC, sC.n)
                P.em("sync", fstat)
                sS.n += 32
                P.wait("sync", sS, sS.n)
            P.run_block(nc, "b1")

        with ExitStack() as b2a:
            wst = [b2a.enter_context(nc.sbuf_tensor(f"wst{i}", [128, 4 * D], BF16)) for i in range(2)]
            wT_sec = b2a.enter_context(nc.sbuf_tensor("wT_sec", [128, DC * D], BF16))
            vtc = [b2a.enter_context(nc.sbuf_tensor(f"vtc{i}", [128, T], BF16)) for i in range(2)]
            psW = [b2a.enter_context(nc.psum_tensor(f"psW{i}", [128, D], BF16)) for i in range(2)]
            psQ = [b2a.enter_context(nc.psum_tensor(f"psQ{i}", [128, 512], F32)) for i in range(4)]

            P = Prog()

            def f(E):
                mm = None
                for dc in range(DC):
                    mm = E.activation(htT[:, bass.ts(dc, T)], htT[:, bass.ts(dc, T)], AF.Identity,
                                      bias=nb_c[:, dc:dc + 1], scale=nw_c[:, dc:dc + 1])
                mm.then_inc(sC.h, 1)
            P.em("scalar", f)
            sC.n += 1
            hT_ready = sC.n

            wst_guard = [0, 0]
            psW_guard = [0, 0]
            psQ_guard_v = [0, 0, 0, 0]
            psQ_guard_c = [0, 0, 0, 0]
            spill_guard = [0, 0]
            for sec in range(3):
                for hf in range(2):
                    b = hf % 2
                    P.wait("gpsimd", sT, wst_guard[b])

                    def f(E, sec=sec, hf=hf, b=b):
                        rows0 = sec * D + hf * 512
                        src = qkvw_d.ap()[rows0:rows0 + 512, :].rearrange(
                            "(oc p) d -> p oc d", p=128)
                        dst = wst[b][:].rearrange("p (oc d) -> p oc d", oc=4)
                        E.dma_start(out=dst, in_=src).then_inc(sWS[b].h, 16)
                    P.em("gpsimd", f)
                    sWS[b].n += 16

                    P.wait("tensor", sWS[b], sWS[b].n)
                    for oc in range(4):
                        pr = oc % 2
                        P.wait("tensor", sV, psW_guard[pr])

                        def f(E, b=b, oc=oc, pr=pr):
                            mm = None
                            for dc in range(DC):
                                mm = E.transpose(psW[pr][:, bass.ts(dc, 128)],
                                                 wst[b][:, oc * D + dc * 128:oc * D + (dc + 1) * 128],
                                                 idb[:])
                            mm.then_inc(sT.h, 1)
                        P.em("tensor", f)
                        sT.n += 1
                        if oc == 3:
                            wst_guard[b] = sT.n

                        P.wait("vector", sT, sT.n)

                        def f(E, pr=pr, hf=hf, oc=oc):
                            oabs = hf * 4 + oc
                            E.tensor_copy(
                                wT_sec[:].rearrange("p (j o) -> p j o", j=DC)[:, :, bass.ts(oabs, 128)],
                                psW[pr][:].rearrange("p (j c) -> p j c", j=DC)).then_inc(sV.h, 1)
                        P.em("vector", f)
                        sV.n += 1
                        psW_guard[pr] = sV.n
                wT_ready_v = sV.n

                P.wait("tensor", sV, wT_ready_v)
                P.wait("tensor", sC, hT_ready)
                if sec < 2:
                    dst_sb = q_sb if sec == 0 else k_sb
                    bias_bc = qb_bc if sec == 0 else kb_bc
                    for i in range(NT):
                        for oh in range(2):
                            pq = (i * 2 + oh) % 4
                            P.wait("tensor", sV, psQ_guard_v[pq])

                            def f(E, i=i, oh=oh, pq=pq):
                                mm = None
                                for dc in range(DC):
                                    mm = E.matmul(psQ[pq][:, :],
                                                  htT[:, dc * T + i * 128:dc * T + (i + 1) * 128],
                                                  wT_sec[:, dc * D + oh * 512:dc * D + (oh + 1) * 512],
                                                  start=(dc == 0), stop=(dc == DC - 1))
                                mm.then_inc(sT.h, 1)
                            P.em("tensor", f)
                            sT.n += 1

                            P.wait("vector", sT, sT.n)

                            def f(E, i=i, oh=oh, pq=pq, dst_sb=dst_sb, bias_bc=bias_bc):
                                E.scalar_tensor_tensor(
                                    dst_sb[:, i * D + oh * 512:i * D + (oh + 1) * 512],
                                    psQ[pq][:], 1.0, bias_bc[:, bass.ts(oh, 512)],
                                    ALU.mult, ALU.add).then_inc(sV.h, 1)
                            P.em("vector", f)
                            sV.n += 1
                            psQ_guard_v[pq] = sV.n
                else:
                    P.wait("tensor", sV, max(psQ_guard_v))
                    for kkc in range(DC):
                        vb = kkc % 2
                        for ts in range(NTS):
                            pq = ts % 4
                            P.wait("tensor", sC, psQ_guard_c[pq])

                            def f(E, kkc=kkc, ts=ts, pq=pq):
                                mm = None
                                for dc in range(DC):
                                    mm = E.matmul(psQ[pq][:, :],
                                                  wT_sec[:, dc * D + kkc * 128:dc * D + (kkc + 1) * 128],
                                                  htT[:, dc * T + ts * 512:dc * T + (ts + 1) * 512],
                                                  start=(dc == 0), stop=(dc == DC - 1))
                                mm.then_inc(sT.h, 1)
                            P.em("tensor", f)
                            sT.n += 1

                            P.wait("scalar", sT, sT.n)
                            P.wait("scalar", sSP[vb], spill_guard[vb])

                            def f(E, kkc=kkc, ts=ts, pq=pq, vb=vb):
                                E.activation(vtc[vb][:, bass.ts(ts, 512)], psQ[pq][:], AF.Identity,
                                             bias=vb_c[:, kkc:kkc + 1], scale=1.0).then_inc(sC.h, 1)
                            P.em("scalar", f)
                            sC.n += 1
                            psQ_guard_c[pq] = sC.n

                        P.wait("sync", sC, sC.n)

                        def f(E, kkc=kkc, vb=vb):
                            E.dma_start(out=vspill_d.ap()[:, bass.ts(kkc, T)],
                                        in_=vtc[vb][:]).then_inc(sSP[vb].h, 16)
                        P.em("sync", f)
                        sSP[vb].n += 16
                        spill_guard[vb] = sSP[vb].n

            dump(P, "d_htT", htT[:])
            dump(P, "d_q", q_sb[:])
            dump(P, "d_k", k_sb[:])
            if debug:
                P.wait("sync", sSP[0], sSP[0].n)
                P.wait("sync", sSP[1], sSP[1].n)
                dump(P, "d_v", vspill_d.ap()[:, :])
            P.run_block(nc, "b2a")

        with ExitStack() as b2b:
            lb = [b2b.enter_context(nc.sbuf_tensor(f"lb{i}", [128, D], F32)) for i in range(2)]
            psL = [b2b.enter_context(nc.psum_tensor(f"psL{i}", [128, D], F32)) for i in range(4)]

            P = Prog()
            lb_guard = [0, 0]
            for p in range(2):
                for qc in range(4):
                    def f(E, p=p, qc=qc):
                        mm = None
                        for i in range(NT):
                            for kh in range(2):
                                mm = E.matmul(psL[qc][:, bass.ts(kh, 512)],
                                              q_sb[:, i * D + p * 512 + qc * 128:
                                                   i * D + p * 512 + (qc + 1) * 128],
                                              k_sb[:, i * D + kh * 512:i * D + (kh + 1) * 512],
                                              start=(i == 0), stop=(i == NT - 1))
                        mm.then_inc(sT.h, 1)
                    P.em("tensor", f)
                    sT.n += 1

                    b = qc % 2
                    P.wait("vector", sT, sT.n)
                    P.wait("vector", sLB[b], lb_guard[b])

                    def f(E, qc=qc, b=b):
                        E.tensor_copy(lb[b][:], psL[qc][:]).then_inc(sV.h, 1)
                    P.em("vector", f)
                    sV.n += 1

                    P.wait("sync", sV, sV.n)

                    def f(E, p=p, qc=qc, b=b):
                        r0 = p * 512 + qc * 128
                        E.dma_start(out=partl_d.ap()[r0:r0 + 128, :], in_=lb[b][:]).then_inc(sLB[b].h, 16)
                    P.em("sync", f)
                    sLB[b].n += 16
                    lb_guard[b] = sLB[b].n
                if p == 0:
                    P.wait("tensor", sV, sV.n)

            P.wait("gpsimd", sLB[0], sLB[0].n)
            P.wait("gpsimd", sLB[1], sLB[1].n)

            def f(E):
                E.collective_compute(
                    "AllReduce", ALU.add,
                    replica_groups=[[0, 1, 2, 3], [4, 5, 6, 7]],
                    ins=[partl_d.ap().opt()],
                    outs=[redl_d.ap().opt()],
                ).then_inc(sCC.h, 1)
            P.em("gpsimd", f)
            sCC.n += 1

            dump(P, "d_partl", partl_d.ap()[:, :])
            P.run_block(nc, "b2b")

        s12.close()

        accum = top.enter_context(nc.sbuf_tensor("accum", [128, NT * D], F32, side="right"))
        with ExitStack() as b3:
            hT = b3.enter_context(nc.sbuf_tensor("hT", [128, HCB * T], BF16))
            w2b = [b3.enter_context(nc.sbuf_tensor(f"w2b{i}", [128, HCB * D], BF16)) for i in range(2)]
            w1c = [b3.enter_context(nc.sbuf_tensor(f"w1c{i}", [128, DC * 128], BF16)) for i in range(2)]
            psH = [b3.enter_context(nc.psum_tensor(f"psH{i}", [128, 512], F32)) for i in range(4)]
            psY = [b3.enter_context(nc.psum_tensor(f"psY{i}", [128, D], F32)) for i in range(2)]

            P = Prog()
            NSUB = T // 512
            psY_guard = [0, 0]
            psH_guard = [0, 0, 0, 0]
            w1_guard = [0, 0]
            w2_guard = [0, 0]
            accum_guard = [0] * NT

            for tt in range(NT):
                py = tt % 2
                P.wait("tensor", sV, psY_guard[py])

                def f(E, tt=tt, py=py):
                    mm = None
                    for dh in range(2):
                        mm = E.matmul(psY[py][:, bass.ts(dh, 512)], combT[:, bass.ts(tt, 128)],
                                      b2_sb[:, bass.ts(dh, 512)], start=True, stop=True)
                    mm.then_inc(sT.h, 1)
                P.em("tensor", f)
                sT.n += 1

                P.wait("vector", sT, sT.n)

                def f(E, tt=tt, py=py):
                    E.tensor_copy(accum[:, bass.ts(tt, D)], psY[py][:]).then_inc(sV.h, 1)
                P.em("vector", f)
                sV.n += 1
                psY_guard[py] = sV.n
                accum_guard[tt] = sV.n

            for e in range(NEXP):
                for hb in range(NHCB):
                    blk = e * NHCB + hb
                    wb2 = blk % 2
                    P.wait("gpsimd", sT, w2_guard[wb2])

                    def f(E, e=e, hb=hb, wb2=wb2):
                        r0 = hb * HCB * 128
                        src = w2_d.ap()[e][r0:r0 + HCB * 128, :].rearrange(
                            "(hc p) d -> p hc d", p=128)
                        dst = w2b[wb2][:].rearrange("p (hc d) -> p hc d", hc=HCB)
                        E.dma_start(out=dst, in_=src).then_inc(sW2[wb2].h, 16)
                    P.em("gpsimd", f)
                    sW2[wb2].n += 16
                    w2_ld = sW2[wb2].n

                    for hc in range(HCB):
                        wb = hc % 2
                        P.wait("gpsimd", sT, w1_guard[wb])

                        def f(E, e=e, hb=hb, hc=hc, wb=wb):
                            c0 = hb * HCB * 128 + hc * 128
                            src = w1_d.ap()[e][:, c0:c0 + 128].rearrange(
                                "(dc p) h -> p dc h", p=128)
                            dst = w1c[wb][:].rearrange("p (dc h) -> p dc h", dc=DC)
                            E.dma_start(out=dst, in_=src).then_inc(sW1[wb].h, 16)
                        P.em("gpsimd", f)
                        sW1[wb].n += 16

                        P.wait("tensor", sW1[wb], sW1[wb].n)
                        for sub in range(NSUB):
                            ph = (hc * NSUB + sub) % 4
                            P.wait("tensor", sC, psH_guard[ph])
                            if b3_serial:
                                P.wait("tensor", sC, sC.n)
                                P.wait("tensor", sV, sV.n)

                            def f(E, hc=hc, sub=sub, ph=ph, wb=wb):
                                mm = None
                                for dc in range(DC):
                                    mm = E.matmul(psH[ph][:, :], w1c[wb][:, bass.ts(dc, 128)],
                                                  xnT[:, dc * T + sub * 512:dc * T + (sub + 1) * 512],
                                                  start=(dc == 0), stop=(dc == DC - 1))
                                mm.then_inc(sT.h, 1)
                            P.em("tensor", f)
                            sT.n += 1
                            if sub == NSUB - 1:
                                w1_guard[wb] = sT.n

                            P.wait("scalar", sT, sT.n)
                            if b3_serial:
                                P.wait("scalar", sC, sC.n)

                            def f(E, e=e, hb=hb, hc=hc, sub=sub, ph=ph):
                                bcol = e * NHC + hb * HCB + hc
                                E.activation(hT[:, hc * T + sub * 512:hc * T + (sub + 1) * 512],
                                             psH[ph][:], AF.Silu, bias=b1c[:, bcol:bcol + 1],
                                             scale=1.0).then_inc(sC.h, 1)
                            P.em("scalar", f)
                            sC.n += 1
                            psH_guard[ph] = sC.n

                    P.wait("tensor", sC, sC.n)
                    P.wait("tensor", sW2[wb2], w2_ld)
                    for tt in range(NT):
                        py = tt % 2
                        P.wait("tensor", sV, psY_guard[py])
                        if b3_serial:
                            P.wait("tensor", sV, sV.n)
                            P.wait("tensor", sC, sC.n)

                        def f(E, tt=tt, py=py, wb2=wb2):
                            mm = None
                            for hc in range(HCB):
                                for dh in range(2):
                                    mm = E.matmul(psY[py][:, bass.ts(dh, 512)],
                                                  hT[:, hc * T + tt * 128:hc * T + (tt + 1) * 128],
                                                  w2b[wb2][:, hc * D + dh * 512:hc * D + (dh + 1) * 512],
                                                  start=(hc == 0), stop=(hc == HCB - 1))
                            mm.then_inc(sT.h, 1)
                        P.em("tensor", f)
                        sT.n += 1

                        P.wait("vector", sT, sT.n)
                        P.wait("vector", sV, accum_guard[tt])
                        if b3_serial:
                            P.wait("vector", sV, sV.n)

                        def f(E, e=e, tt=tt, py=py):
                            E.scalar_tensor_tensor(accum[:, bass.ts(tt, D)], psY[py][:],
                                                   comb[:, tt * NEXP + e:tt * NEXP + e + 1],
                                                   accum[:, bass.ts(tt, D)],
                                                   ALU.mult, ALU.add).then_inc(sV.h, 1)
                        P.em("vector", f)
                        sV.n += 1
                        psY_guard[py] = sV.n
                        accum_guard[tt] = sV.n
                    w2_guard[wb2] = sT.n

            dump(P, "d_accum", accum[:])
            if debug:
                P.wait("sync", sCC, 1)
                dump(P, "d_redl", redl_d.ap()[:, :])
            P.run_block(nc, "b3")

        s13.close()

        with ExitStack() as b4:
            vT = b4.enter_context(nc.sbuf_tensor("vT", [128, DC * T], BF16))
            wT = b4.enter_context(nc.sbuf_tensor("wT", [128, DC * D], BF16))
            aT = b4.enter_context(nc.sbuf_tensor("aT", [128, DC * T], BF16))
            pwT = b4.enter_context(nc.sbuf_tensor("pwT", [128, DC * D], BF16))
            pst = b4.enter_context(nc.sbuf_tensor("pst", [128, 2 * D], BF16))
            lc = [b4.enter_context(nc.sbuf_tensor(f"lc{i}", [128, D], F32)) for i in range(2)]
            wbf = b4.enter_context(nc.sbuf_tensor("wbf", [128, D], BF16))
            x2 = [b4.enter_context(nc.sbuf_tensor(f"x2{i}", [128, D], F32)) for i in range(2)]
            xn2 = b4.enter_context(nc.sbuf_tensor("xn2", [128, D], F32))
            gbuf = b4.enter_context(nc.sbuf_tensor("gbuf", [128, D], F32))
            obuf = [b4.enter_context(nc.sbuf_tensor(f"obuf{i}", [128, D], F32)) for i in range(2)]
            sstat = b4.enter_context(nc.sbuf_tensor("sstat", [128, 4], F32))
            psW2 = [b4.enter_context(nc.psum_tensor(f"psW2{i}", [128, D], BF16)) for i in range(2)]
            psA2 = [b4.enter_context(nc.psum_tensor(f"psA2{i}", [128, 512], F32)) for i in range(2)]
            psP = [b4.enter_context(nc.psum_tensor(f"psP{i}", [128, D], F32)) for i in range(2)]

            P = Prog()

            P.wait("sync", sSP[0], sSP[0].n)
            P.wait("sync", sSP[1], sSP[1].n)

            def f(E):
                E.dma_start(out=vT[:], in_=vspill_d.ap()[:, :]).then_inc(sVTL.h, 16)
            P.em("sync", f)
            sVTL.n += 16

            P.wait("sync", sCC, 1)
            V = lambda b: P.eop("vector", sV, b)
            S = lambda b: P.eop("scalar", sC, b)
            lc_guard = [0, 0]
            psW2_guard = [0, 0]
            wbf_guard = [0]
            for qc in range(DC):
                b = qc % 2
                P.wait("sync", sV, lc_guard[b])

                def f(E, qc=qc, b=b):
                    E.dma_start(out=lc[b][:], in_=redl_d.ap()[bass.ts(qc, 128), :]).then_inc(sLC[b].h, 16)
                P.em("sync", f)
                sLC[b].n += 16

                P.wait("vector", sLC[b], sLC[b].n)
                P.wait("vector", sC, sC.n)
                V(lambda E, b=b: E.tensor_reduce(sstat[:, 0:1], lc[b][:], AX, ALU.max))
                V(lambda E: E.tensor_scalar(sstat[:, 0:1], sstat[:, 0:1], -seq_scale, None, ALU.mult))

                P.wait("scalar", sV, sV.n)
                S(lambda E, b=b: E.activation(lc[b][:], lc[b][:], AF.Exp, bias=sstat[:, 0:1], scale=seq_scale))

                P.wait("vector", sC, sC.n)
                V(lambda E, b=b: E.tensor_reduce(sstat[:, 1:2], lc[b][:], AX, ALU.add))
                V(lambda E: E.reciprocal(sstat[:, 1:2], sstat[:, 1:2]))
                P.wait("vector", sT, wbf_guard[0])
                V(lambda E, b=b: E.tensor_scalar(wbf[:], lc[b][:], sstat[:, 1:2], None, ALU.mult))
                lc_guard[b] = sV.n

                pb = qc % 2
                P.wait("tensor", sV, sV.n)
                P.wait("tensor", sV, psW2_guard[pb])

                def f(E, pb=pb):
                    mm = None
                    for kkc in range(DC):
                        mm = E.transpose(psW2[pb][:, bass.ts(kkc, 128)],
                                         wbf[:, bass.ts(kkc, 128)], idb[:])
                    mm.then_inc(sT.h, 1)
                P.em("tensor", f)
                sT.n += 1
                wtr = sT.n
                wbf_guard[0] = wtr

                P.wait("vector", sT, wtr)
                V(lambda E, pb=pb, qc=qc: E.tensor_copy(
                    wT[:].rearrange("p (j q) -> p j q", j=DC)[:, :, bass.ts(qc, 128)],
                    psW2[pb][:].rearrange("p (j c) -> p j c", j=DC)))
                psW2_guard[pb] = sV.n
            wT_done_v = sV.n

            pst_guard = 0
            for hf in range(4):
                P.wait("gpsimd", sT, pst_guard)

                def f(E, hf=hf):
                    src = pw_d.ap()[hf * 256:(hf + 1) * 256, :].rearrange(
                        "(oc p) q -> p oc q", p=128)
                    dst = pst[:].rearrange("p (oc q) -> p oc q", oc=2)
                    E.dma_start(out=dst, in_=src).then_inc(sPST.h, 16)
                P.em("gpsimd", f)
                sPST.n += 16

                P.wait("tensor", sPST, sPST.n)
                for oc in range(2):
                    pb = oc % 2
                    P.wait("tensor", sV, psW2_guard[pb])

                    def f(E, oc=oc, pb=pb):
                        mm = None
                        for qc in range(DC):
                            mm = E.transpose(psW2[pb][:, bass.ts(qc, 128)],
                                             pst[:, oc * D + qc * 128:oc * D + (qc + 1) * 128],
                                             idb[:])
                        mm.then_inc(sT.h, 1)
                    P.em("tensor", f)
                    sT.n += 1
                    if oc == 1:
                        pst_guard = sT.n

                    P.wait("vector", sT, sT.n)
                    P.eop("vector", sV, lambda E, hf=hf, oc=oc, pb=pb: E.tensor_copy(
                        pwT[:].rearrange("p (j o) -> p j o", j=DC)[:, :, bass.ts(hf * 2 + oc, 128)],
                        psW2[pb][:].rearrange("p (j c) -> p j c", j=DC)))
                    psW2_guard[pb] = sV.n
            pwT_done_v = sV.n

            P.wait("tensor", sVTL, sVTL.n)
            P.wait("tensor", sV, wT_done_v)
            psA2_guard = [0, 0]
            for qc in range(DC):
                for ts in range(NTS):
                    pa = (qc * NTS + ts) % 2
                    P.wait("tensor", sC, psA2_guard[pa])

                    def f(E, qc=qc, ts=ts, pa=pa):
                        mm = None
                        for kkc in range(DC):
                            mm = E.matmul(psA2[pa][:, :],
                                          wT[:, kkc * D + qc * 128:kkc * D + (qc + 1) * 128],
                                          vT[:, kkc * T + ts * 512:kkc * T + (ts + 1) * 512],
                                          start=(kkc == 0), stop=(kkc == DC - 1))
                        mm.then_inc(sT.h, 1)
                    P.em("tensor", f)
                    sT.n += 1

                    P.wait("scalar", sT, sT.n)

                    def f(E, qc=qc, ts=ts, pa=pa):
                        E.copy(aT[:, qc * T + ts * 512:qc * T + (ts + 1) * 512],
                               psA2[pa][:]).then_inc(sC.h, 1)
                    P.em("scalar", f)
                    sC.n += 1
                    psA2_guard[pa] = sC.n
            aT_done_c = sC.n

            P.wait("tensor", sC, aT_done_c)
            P.wait("tensor", sV, pwT_done_v)
            x2_guard = [0, 0]
            psP_guard = [0, 0]
            obuf_guard = [0, 0]
            prev_o = 0
            for i in range(NT):
                r = i % 2
                P.wait("sync", sV, x2_guard[r])

                def f(E, i=i, r=r):
                    E.dma_start(out=x2[r][:], in_=x_d[bass.ts(i, 128), :]).then_inc(sX2[r].h, 16)
                P.em("sync", f)
                sX2[r].n += 16
                x_ld = sX2[r].n

                P.wait("tensor", sV, psP_guard[r])

                def f(E, i=i, r=r):
                    mm = None
                    for qc in range(DC):
                        for dh in range(2):
                            mm = E.matmul(psP[r][:, bass.ts(dh, 512)],
                                          aT[:, qc * T + i * 128:qc * T + (i + 1) * 128],
                                          pwT[:, qc * D + dh * 512:qc * D + (dh + 1) * 512],
                                          start=(qc == 0), stop=(qc == DC - 1))
                    mm.then_inc(sT.h, 1)
                P.em("tensor", f)
                sT.n += 1
                pmm = sT.n

                P.wait("scalar", sX2[r], x_ld)
                P.wait("scalar", sV, prev_o)

                S(lambda E, i=i, r=r: E.activation(xn2[:], x2[r][:], AF.Identity,
                                                   bias=nbias1[:, i:i + 1], scale=rstd1[:, i:i + 1]))
                S(lambda E, i=i: E.activation(gbuf[:], accum[:, bass.ts(i, D)], AF.Silu,
                                              bias=0.0, scale=1.0))

                P.wait("vector", sT, pmm)
                P.wait("vector", sC, sC.n)
                P.wait("vector", sO[r], obuf_guard[r])
                V(lambda E, r=r: E.scalar_tensor_tensor(xn2[:], psP[r][:], 1.0, xn2[:], ALU.mult, ALU.add))
                V(lambda E: E.tensor_tensor(xn2[:], xn2[:], projb_bc[:], ALU.add))
                V(lambda E: E.tensor_tensor(xn2[:], xn2[:], gbuf[:], ALU.mult))
                V(lambda E, r=r: E.tensor_tensor(obuf[r][:], xn2[:], x2[r][:], ALU.add))
                o_done = sV.n
                psP_guard[r] = o_done
                x2_guard[r] = o_done
                prev_o = o_done

                P.wait("sync", sV, o_done)

                def f(E, i=i, r=r):
                    E.dma_start(out=y_d[bass.ts(i, 128), :], in_=obuf[r][:]).then_inc(sO[r].h, 16)
                P.em("sync", f)
                sO[r].n += 16
                obuf_guard[r] = sO[r].n

            P.wait("sync", sO[0], sO[0].n)
            P.wait("sync", sO[1], sO[1].n)

            P.run_block(nc, "b4")

    return nc


def kernel(**inputs):
    x = np.ascontiguousarray(inputs["L0_attn_x"])
    B, L, _ = x.shape
    n_cores = 8
    per_batch = n_cores // B
    Tc = L // per_batch

    weights = {
        "qkv_w": np.ascontiguousarray(inputs["L0_attn_qkv_w"], dtype=np.float32),
        "qkv_b": np.ascontiguousarray(inputs["L0_attn_qkv_b"], dtype=np.float32),
        "norm_w": np.ascontiguousarray(inputs["L0_attn_norm_w"], dtype=np.float32),
        "norm_b": np.ascontiguousarray(inputs["L0_attn_norm_b"], dtype=np.float32),
        "proj_w": np.ascontiguousarray(inputs["L0_attn_proj_w"], dtype=np.float32),
        "proj_b": np.ascontiguousarray(inputs["L0_attn_proj_b"], dtype=np.float32),
        "gate_w": np.ascontiguousarray(inputs["L0_ffn_gate_w"], dtype=np.float32),
        "gate_b": np.ascontiguousarray(inputs["L0_ffn_gate_b"], dtype=np.float32),
        "w1": np.ascontiguousarray(inputs["L0_ffn_w1"], dtype=np.float32),
        "b1": np.ascontiguousarray(inputs["L0_ffn_b1"], dtype=np.float32),
        "w2": np.ascontiguousarray(inputs["L0_ffn_w2"], dtype=np.float32),
        "b2": np.ascontiguousarray(inputs["L0_ffn_b2"], dtype=np.float32),
        "ident_bf": np.eye(128, dtype=ml_dtypes.bfloat16),
        "ident_f32": np.eye(128, dtype=np.float32),
    }

    nc = build_nc(Tc // 128, 1.0 / math.sqrt(L))

    in_maps = []
    for c in range(n_cores):
        b = c // per_batch
        s0 = (c % per_batch) * Tc
        m = dict(weights)
        m["x"] = np.ascontiguousarray(x[b, s0:s0 + Tc, :], dtype=np.float32)
        in_maps.append(m)

    res = run_bass_kernel_spmd(nc, in_maps, core_ids=list(range(n_cores)))
    out = np.empty((B, L, D), dtype=np.float32)
    for c in range(n_cores):
        b = c // per_batch
        s0 = (c % per_batch) * Tc
        out[b, s0:s0 + Tc, :] = res.results[c]["y"]
    return out
